# version 16
# baseline (speedup 1.0000x reference)
import dataclasses
import numpy as np
from contextlib import ExitStack

import concourse.bass as bass
import concourse.mybir as mybir
from concourse.bass_utils import run_bass_kernel_spmd

F32 = mybir.dt.float32
BF16 = mybir.dt.bfloat16
AF = mybir.ActivationFunctionType
ALU = mybir.AluOpType
AX = mybir.AxisListType

D = 2048
NCH = 16
H = 16
B = 2
T = 4096
TC = 1024
TP = 3072
L = 32
NGO = TC // L
NGP = TP // L
HID = 8192
NH = 64
DS = 2816
NSH = 22
DE = 1408
NEH = 11
E = 8
EPS = 1e-5

def bcast(ap, dim, count):
    l = [list(x) for x in ap.ap]
    l.insert(dim, [0, count])
    return dataclasses.replace(ap, ap=l)


COMPUTE = ("tensor", "vector", "scalar")
DMAQ = ("sync", "gpsimd")
ENGINES = COMPUTE + DMAQ


class Res:
    __slots__ = ("w", "rs", "name")

    def __init__(self, name=""):
        self.w = None
        self.rs = []
        self.name = name


class Plan:

    def __init__(self):
        self.ins = {e: [] for e in ENGINES}
        self.floor = []
        self.dmacnt = {}
        self.dmakey = {}
        self.dram_keys = set()
        self.drain_gates = False

    def op(self, eng, builder, reads=(), writes=(), extra_waits=(), dma=None):
        deps = set(self.floor)
        for r in reads:
            if r.w is not None:
                deps.add(r.w)
        for wres in writes:
            if wres.w is not None:
                deps.add(wres.w)
            for t in wres.rs:
                deps.add(t)
        for t in extra_waits:
            if t is not None:
                deps.add(t)
        idx = len(self.ins[eng])
        tok = (eng, idx)
        waits = []
        seen_keys = {}
        for (e2, i2) in deps:
            key = self.dmakey.get((e2, i2))
            if key is not None:
                seen_keys[key] = self.dmacnt[key]
            else:
                waits.append(("C", e2, i2))
        for key, cnt in seen_keys.items():
            waits.append(("D", key, cnt))
        if dma is not None:
            self.dmacnt[dma] = self.dmacnt.get(dma, 0) + 1
            self.dmakey[tok] = dma
        self.ins[eng].append([builder, waits, dma is not None, dma])
        for r in reads:
            r.rs.append(tok)
        for wres in writes:
            wres.w = tok
            wres.rs = []
        return tok

    def gate(self):
        def last(e):
            return (e, len(self.ins[e]) - 1) if self.ins[e] else None
        tv = self.op("vector", lambda v: v.engine_nop(),
                     extra_waits=[last("vector")])
        ts_ = self.op("scalar", lambda s: s.nop(), extra_waits=[last("scalar")])
        tg = self.op("gpsimd", lambda g: g.nop(), extra_waits=[last("gpsimd")])
        tt = self.op("tensor", lambda t: t.nop(),
                     extra_waits=[tv, ts_, tg, last("tensor"), last("sync")])
        self.ins["tensor"][tt[1]][1].extend(
            ("D", k, c) for k, c in self.dmacnt.items()
            if self.drain_gates or k not in self.dram_keys)
        self.floor = [tt]
        return tt

    def finalize(self, nc, ctx):
        csem = {e: ctx.enter_context(nc.semaphore("c_" + e)) for e in ENGINES}
        dsem = {k: ctx.enter_context(nc.semaphore("k_" + k)) for k in self.dmacnt}
        for e in ENGINES:
            for ent in self.ins[e]:
                for w in ent[1]:
                    if w[0] == "C":
                        self.ins[w[1]][w[2]][2] = True
        val = {}
        for e in ENGINES:
            cc = 0
            for i, ent in enumerate(self.ins[e]):
                if ent[2] and ent[3] is None:
                    cc += 1
                val[(e, i)] = cc
        out = {e: [] for e in ENGINES}
        for e in ENGINES:
            hw = {}
            for builder, waits, flagged, key in self.ins[e]:
                byk = {}
                for w in waits:
                    if w[0] == "C":
                        k2 = ("C", w[1])
                        v = val[(w[1], w[2])]
                    else:
                        k2 = ("D", w[1])
                        v = w[2] * 16
                    if v > byk.get(k2, 0):
                        byk[k2] = v
                resolved = []
                for k2, v in byk.items():
                    if v > hw.get(k2, 0):
                        hw[k2] = v
                        sem = csem[k2[1]] if k2[0] == "C" else dsem[k2[1]]
                        resolved.append((sem, v))
                out[e].append((builder, resolved, flagged, key))
        self.final = out
        self.csem = csem
        self.dsem = dsem

    def run_engine(self, eng, engine):
        for builder, waits, flagged, key in self.final[eng]:
            for s, v in waits:
                engine.wait_ge(s, v)
            ins = builder(engine)
            if key is not None:
                ins.then_inc(self.dsem[key], 16)
            elif flagged:
                ins.then_inc(self.csem[eng], 1)


class Scope:

    def __init__(self, nc):
        self.nc = nc
        self.cms = []

    def sb(self, name, shape, dt, side=None):
        cm = self.nc.sbuf_tensor(name, list(shape), dt, side=side)
        h = cm.__enter__()
        self.cms.append((cm, side))
        return h

    def psum(self, name, shape, dt=F32):
        cm = self.nc.psum_tensor(name, list(shape), dt)
        h = cm.__enter__()
        self.cms.append((cm, "psum"))
        return h

    def close_right(self):
        keep = []
        for cm, side in reversed(self.cms):
            if side == "right":
                cm.__exit__(None, None, None)
            else:
                keep.append((cm, side))
        self.cms = list(reversed(keep))

    def close(self):
        for cm, side in reversed(self.cms):
            cm.__exit__(None, None, None)
        self.cms = []


def build_nc(stage="full", taps=()):
    taps = set(taps)
    nc = bass.Bass("TRN2", target_bir_lowering=False, debug=False)

    d_in = {}

    def din(name, shape):
        d_in[name] = nc.dram_tensor(name, list(shape), F32, kind="ExternalInput").ap()
        return d_in[name]

    xo_d = din("xo", (TC // 128, 128, NCH, 128))
    xp_d = din("xp", (TP // 128, 128, NCH, 128))
    din("maskp", (1, NGP))
    for nm in ("ln1w", "ln1b", "ln2w", "ln2b", "inb", "outb", "b2v"):
        din(nm, (128, 16))
    din("b1v", (128, 64))
    in_w_d = din("in_w", (NCH, 128, NCH, 128)); out_w_d = din("out_w", (NCH, 128, NCH, 128))
    w1_d = din("w1", (NH, 128, NCH, 128)); w2_d = din("w2", (NCH, 128, NH, 128))
    wsg_d = din("wsg", (NSH, 128, NCH, 128)); wsu_d = din("wsu", (NSH, 128, NCH, 128))
    wsd_d = din("wsd", (NCH, 128, NSH, 128))
    weg_d = din("weg", (E, NEH, 128, NCH, 128)); weu_d = din("weu", (E, NEH, 128, NCH, 128))
    wed_d = din("wed", (E, NCH, 128, NEH, 128))
    wr_d = din("wr", (D, E))
    Am_d = din("Am", (H, 128, 128)); Bm_d = din("Bm", (H, 128, 128))
    Cm_d = din("Cm", (H, 128, 128)); Dm_d = din("Dm", (H, 128, 128))

    out_d = nc.dram_tensor("out", [TC, D], F32, kind="ExternalOutput").ap()
    tap_d = {}

    def tap(name, shape, dtype=F32):
        if name in taps:
            tap_d[name] = nc.dram_tensor("tap_" + name, list(shape), dtype,
                                         kind="ExternalOutput").ap()
        return tap_d.get(name)

    sx2_d = nc.dram_tensor("sx2", [128, NCH, TC], F32, kind="Internal").ap()
    ssp_d = nc.dram_tensor("ssp", [128, NCH, TC], F32, kind="Internal").ap()

    ctx = ExitStack()
    P = Plan()
    P.drain_gates = bool(taps)
    P.dram_keys = {"sx2", "ssp", "out", "tap1", "tap2", "tap3", "tap4",
                   "tap5", "tap6", "tap7"}
    perm = Scope(nc)

    R = {}

    def res(name):
        if name not in R:
            R[name] = Res(name)
        return R[name]

    ones_bf = perm.sb("ones_bf", [128, 128], BF16)
    ident_bf = perm.sb("ident_bf", [128, 128], BF16)
    ident_f = perm.sb("ident_f", [128, 128], F32)
    lnv = perm.sb("lnv", [128, 16 * 6], F32)
    b1s = perm.sb("b1s", [128, 64], F32)
    b2s = perm.sb("b2s", [128, 16], F32)
    maskb = perm.sb("maskb", [128, NGP], F32)
    mask1 = perm.sb("mask1", [1, NGP], F32)
    sqbf = perm.sb("sqbf", [128, 512], BF16)
    r_sqbf = res("sqbf")

    LVOFF = {"ln1w": 0, "ln1b": 16, "ln2w": 32, "ln2b": 48, "inb": 64, "outb": 80}

    def lncol(nm, m):
        return lnv[:, LVOFF[nm] + m:LVOFF[nm] + m + 1]

    r_const = res("const")
    P.op("gpsimd", lambda g: g.memset(ones_bf[:], 1.0), writes=[r_const])
    P.op("gpsimd", lambda g: g.memset(ident_bf[:], 0.0), writes=[r_const])
    P.op("gpsimd", lambda g: g.affine_select(out=ident_bf[:], in_=ident_bf[:],
         compare_op=ALU.not_equal, fill=1.0, base=0, pattern=[[-1, 128]],
         channel_multiplier=1), writes=[r_const])
    P.op("gpsimd", lambda g: g.memset(ident_f[:], 0.0), writes=[r_const])
    P.op("gpsimd", lambda g: g.affine_select(out=ident_f[:], in_=ident_f[:],
         compare_op=ALU.not_equal, fill=1.0, base=0, pattern=[[-1, 128]],
         channel_multiplier=1), writes=[r_const])

    r_vecs = res("lnvecs")
    for nm, dst in (("ln1w", lnv[:, 0:16]), ("ln1b", lnv[:, 16:32]),
                    ("ln2w", lnv[:, 32:48]), ("ln2b", lnv[:, 48:64]),
                    ("inb", lnv[:, 64:80]), ("outb", lnv[:, 80:96]),
                    ("b1v", b1s[:]), ("b2v", b2s[:]), ("maskp", mask1[:])):
        P.op("sync", lambda s, d=dst, src=d_in[nm]: s.dma_start(out=d, in_=src),
             writes=[r_vecs], dma="vecs")

    usc = Scope(nc)
    u_own = usc.sb("u_own", [128, H, L, NGO], BF16)
    u_prec = usc.sb("u_prec", [128, H, L, NGP], BF16)

    fsc = Scope(nc)
    mmP = [fsc.psum(f"mm{i}", [128, 512], F32) for i in range(2)]
    stP_s = fsc.psum("stP_s", [128, 128], F32)
    stP_q = fsc.psum("stP_q", [128, 128], F32)
    r_mmP = [res("mm0"), res("mm1")]
    r_stP = res("stP")

    r_maskb = res("maskb")
    P.op("vector", lambda v: v.tensor_copy(sqbf[0:1, 0:NGP], mask1[:]),
         reads=[r_vecs], writes=[r_maskb, r_sqbf])
    P.op("tensor", lambda t: t.matmul(stP_s[:, 0:NGP], ones_bf[0:1, :], sqbf[0:1, 0:NGP],
         start=True, stop=True, skip_group_check=True),
         reads=[r_maskb, r_const, r_sqbf], writes=[r_stP])
    P.op("vector", lambda v: v.tensor_copy(maskb[:], stP_s[:, 0:NGP]),
         reads=[r_stP], writes=[r_maskb])

    wr16 = [fsc.sb(f"wr16_{i}", [128, 16, 128], BF16) for i in range(4)]
    r_wr16 = [res(f"wr16_{i}") for i in range(4)]
    class WStream:

        def __init__(self, rbufs, rres, w_list, ahead=None):
            self.bufs, self.res, self.w = rbufs, rres, list(w_list)
            self.R = len(rbufs)
            self.ahead = self.R if ahead is None else ahead
            self.issued = 0

        def _issue(self):
            i = self.issued
            b, r = self.bufs[i % self.R], self.res[i % self.R]
            ap = self.w[i]
            nk = ap.shape[1]
            P.op("gpsimd", lambda g, d=b, ap=ap, nk=nk: g.dma_start(
                out=d[:, 0:nk, :], in_=ap), writes=[r], dma=r.name)
            self.issued += 1

        def get(self, i):
            while self.issued < min(i + self.ahead, len(self.w)):
                self._issue()
            return self.bufs[i % self.R], self.res[i % self.R]

    def stream_wcol(rbufs, rres, w_hbm, nk, m):
        ws = WStream(rbufs, rres, [w_hbm[m]])
        return ws.get(0)

    def proj_sweep(rbufs, rres, w_hbm, kdim, m_list, tiles, mov_fn, mov_res,
                   evac_fn):
        nk = kdim // 128
        ml = list(m_list)
        ws = WStream(rbufs, rres, [w_hbm[m] for m in ml])
        for mi, m in enumerate(ml):
            wsb, wres = ws.get(mi)
            for ti, (t0, tw) in enumerate(tiles):
                pb = (mi * len(tiles) + ti) % len(mmP)
                pst, prs = mmP[pb], r_mmP[pb]
                mres = mov_res(t0) if callable(mov_res) else mov_res
                for k in range(nk):
                    P.op("tensor", lambda t, k=k, pst=pst, t0=t0, tw=tw, wsb=wsb:
                         t.matmul(pst[:, 0:tw], wsb[:, k, :], mov_fn(k, t0, tw),
                                  start=(k == 0), stop=(k == nk - 1),
                                  skip_group_check=True),
                         reads=[wres] + mres,
                         writes=([prs] if k in (0, nk - 1) else []))
                evac_fn(m, pst, prs, ti)

    def stats_chunk(ps_sum, ps_sq, r_ps, xbf_ap, r_x, tw, first, last, sq0=0):
        P.op("scalar", lambda sc: sc.activation(sqbf[:, sq0:sq0 + tw], xbf_ap,
             AF.Square), reads=[r_x], writes=[r_sqbf])
        P.op("tensor", lambda t: t.matmul(ps_sum, ones_bf[:], xbf_ap,
             start=first, stop=last, skip_group_check=True),
             reads=[r_x, r_const], writes=([r_ps] if first else []))
        P.op("tensor", lambda t: t.matmul(ps_sq, ones_bf[:], sqbf[:, sq0:sq0 + tw],
             start=first, stop=last, skip_group_check=True),
             reads=[r_sqbf, r_const], writes=([r_ps] if last else []))

    def stats_finish(ps_sum, ps_sq, r_ps, mean_ap, inv_ap, t1_ap,
                     r_mean, r_inv, r_t1):
        P.op("vector", lambda v: v.tensor_scalar_mul(mean_ap, ps_sum, 1.0 / D),
             reads=[r_ps], writes=[r_mean])
        P.op("vector", lambda v: v.tensor_scalar_mul(t1_ap, ps_sq, 1.0 / D),
             reads=[r_ps], writes=[r_t1])
        P.op("vector", lambda v: v.tensor_mul(inv_ap, mean_ap, mean_ap),
             reads=[r_mean], writes=[r_inv])
        P.op("vector", lambda v: v.tensor_sub(t1_ap, t1_ap, inv_ap),
             reads=[r_inv], writes=[r_t1])
        P.op("vector", lambda v: v.tensor_scalar_add(t1_ap, t1_ap, EPS),
             writes=[r_t1])
        P.op("vector", lambda v: v.reciprocal(inv_ap, t1_ap),
             reads=[r_t1], writes=[r_inv])
        P.op("scalar", lambda s: s.activation(inv_ap, inv_ap, AF.Sqrt),
             writes=[r_inv])

    xt_sweep = fsc.sb("xt_sweep", [128, NCH, TC], BF16, side="right")
    xfm_bf = [fsc.sb(f"xfm_bf{i}", [128, NCH, 128], BF16, side="right")
              for i in range(3)]
    xfm_f = fsc.sb("xfm_f", [128, NCH, 128], F32, side="right")
    mean1 = fsc.sb("mean1", [128, 2, 128], F32, side="right")
    inv1 = fsc.sb("inv1", [128, 2, 128], F32, side="right")
    t1s = fsc.sb("t1s", [128, 128], F32, side="right")
    r_xt_sweep = [res("xt_sweep0"), res("xt_sweep1")]
    r_xtmb = [res("xtmb0"), res("xtmb1")]
    r_xfmb = [res(f"xfmb{i}") for i in range(3)]
    r_xtmf = res("xtmf")
    r_xfmf = res("xfmf")
    r_mean1 = [res("mean1a"), res("mean1b")]
    r_inv1 = [res("inv1a"), res("inv1b")]
    r_t1s = res("t1s")

    r_uo = res("u_own")
    r_up = res("u_prec")

    fr_count = [0]

    def frontend_tile(kind, src_ap, dst_xt, r_dst):
        i = fr_count[0] % 3
        fr_count[0] += 1
        if kind == "prec":
            xbf, r_xbf = xfm_bf[i], r_xfmb[i]
            P.op("gpsimd", lambda g, s=src_ap, d=xbf: g.dma_start(
                out=d[:], in_=s), writes=[r_xbf], dma=r_xbf.name)
        else:
            P.op("sync", lambda s, sa=src_ap: s.dma_start(
                out=xfm_f[:], in_=sa), writes=[r_xfmf], dma="xfmf")
            xbf, r_xbf = xfm_bf[i], r_xfmb[i]
            P.op("gpsimd", lambda g, s=src_ap, d=xbf: g.dma_start(
                out=d[:], in_=s), writes=[r_xbf], dma=r_xbf.name)
        pr = i % 2
        for c in range(NCH):
            stats_chunk(stP_s[:], stP_q[:], r_stP,
                        xbf[:, c, :], r_xbf, 128,
                        first=(c == 0), last=(c == NCH - 1))
        stats_finish(stP_s[:], stP_q[:], r_stP,
                     mean1[:, pr, :], inv1[:, pr, :], t1s[:],
                     r_mean1[pr], r_inv1[pr], r_t1s)
        src_t, r_srct = (xbf, r_xbf) if kind == "prec" else (xfm_f, r_xfmf)
        P.op("vector", lambda v, src_t=src_t, pr=pr: v.tensor_sub(
            src_t[:], src_t[:], bcast(mean1[:, pr, :], 1, NCH)),
            reads=[r_mean1[pr]], writes=[r_srct])
        P.op("vector", lambda v, src_t=src_t, pr=pr: v.tensor_mul(
            src_t[:], src_t[:], bcast(inv1[:, pr, :], 1, NCH)),
            reads=[r_inv1[pr]], writes=[r_srct])
        P.op("vector", lambda v, src_t=src_t: v.tensor_mul(
            src_t[:], src_t[:], bcast(lnv[:, 0:16], 2, 128)),
            reads=[r_vecs], writes=[r_srct])
        P.op("vector", lambda v, src_t=src_t, d=dst_xt: v.tensor_add(
            d[:], src_t[:], bcast(lnv[:, 16:32], 2, 128)),
            reads=[r_srct, r_vecs], writes=[r_dst])

    for sweep in range(4):
        own = (sweep == 3)
        for j in range(8):
            rh = r_xt_sweep[j // 4]
            if own:
                frontend_tile("own", xo_d[j],
                              xt_sweep[:, :, j * 128:(j + 1) * 128], rh)
            else:
                tile = sweep * 8 + j
                frontend_tile("prec", xp_d[tile],
                              xt_sweep[:, :, j * 128:(j + 1) * 128], rh)

        if own:
            def evac(m, pst, prs, ti):
                g0 = (ti * 512) // L
                src = pst[:, 0:512].rearrange("p (g i) -> p i g", i=L)
                P.op("scalar", lambda sc, src=src, m=m, g0=g0: sc.activation(
                    u_own[:, m, :, g0:g0 + 16], src, AF.Identity,
                    bias=lncol("inb", m), scale=1.0),
                    reads=[prs, r_vecs], writes=[r_uo])
        else:
            def evac(m, pst, prs, ti, sweep=sweep):
                g0 = (sweep * 1024 + ti * 512) // L
                src = pst[:, 0:512].rearrange("p (g i) -> p i g", i=L)
                P.op("scalar", lambda sc, src=src, m=m, g0=g0: sc.activation(
                    u_prec[:, m, :, g0:g0 + 16], src, AF.Identity,
                    bias=lncol("inb", m), scale=1.0),
                    reads=[prs, r_vecs], writes=[r_up])

        proj_sweep(wr16, r_wr16, in_w_d, D, range(H), [(0, 512), (512, 512)],
                   lambda k, t0, tw: xt_sweep[:, k, t0:t0 + tw],
                   lambda t0: [r_xt_sweep[t0 // 512]], evac)

    if tap("u_own", (128, H, L, NGO), BF16):
        P.op("sync", lambda s: s.dma_start(out=tap_d["u_own"], in_=u_own[:]),
             reads=[r_uo], dma="tap1")
    if tap("u_prec", (128, H, L, NGP), BF16):
        P.op("sync", lambda s: s.dma_start(out=tap_d["u_prec"], in_=u_prec[:]),
             reads=[r_up], dma="tap2")
    if tap("xt_own", (128, NCH, TC), BF16):
        P.op("sync", lambda s: s.dma_start(out=tap_d["xt_own"], in_=xt_sweep[:]),
             reads=r_xt_sweep, dma="tap3")

    if stage == "uproj":
        P.op("gpsimd", lambda g: g.dma_start(
            out=out_d[0:128, 0:512],
            in_=u_own[:, 0, :, :].rearrange("p i g -> p (i g)")[:, 0:512]),
            reads=[r_uo], dma="out")
        fsc.close()
        usc.close()
        finish(nc, ctx, P, perm)
        return nc, tap_d

    P.gate()
    fsc.close()
    ysc = Scope(nc)
    y_sb = ysc.sb("y_sb", [128, H, TC], BF16, side="right")
    r_y = res("y_sb")
    ssc = Scope(nc)
    phSB = [ssc.psum(f"phS{i}", [128, TC], F32) for i in range(2)]
    pvy = [ssc.psum(f"pvy{i}", [128, 512], F32) for i in range(2)]
    paggB = [ssc.psum(f"pagg{i}", [128, 512], F32) for i in range(2)]
    r_phSB = [res("phS0"), res("phS1")]
    r_pvy = [res("pvy0"), res("pvy1")]
    r_paggB = [res("pagg0"), res("pagg1")]
    trctr = [0]

    abcd = [ssc.sb("abcd0", [128, 4 * 128], BF16)] * 2
    PtBB = [ssc.sb(f"Pt{i}", [128, (L + 1) * 128], BF16) for i in range(2)]
    PPtB1 = ssc.sb("PPt", [128, (L + 1) * 128], BF16)
    trs1 = ssc.sb("trs", [128, 5 * 128], BF16)
    v_pre = [ssc.sb(f"v_pre{i}", [128, L * NGP], BF16) for i in range(2)]
    v_ownb = ssc.sb("v_ownb", [128, TC], BF16)
    h_bf = ssc.sb("h_bf", [128, TC], BF16)
    ntop = [ssc.sb(f"ntop{i}", [128, 128], BF16) for i in range(2)]
    r_ntop = [res("ntop0"), res("ntop1")]
    S_pre = ssc.sb("S_pre", [128, NGP], BF16)
    S_ext = ssc.sb("S_ext", [128, L], BF16)
    Hprev = ssc.sb("Hprev", [128, NGO], BF16)
    Zc = ssc.sb("Zc", [128, 4], BF16)
    r_abcd = [res("abcd0")] * 2
    r_PtB = [res("Pt0"), res("Pt1")]
    r_PPt1 = res("PPt")
    r_trs1 = {nm: res("trs_" + nm) for nm in ("Bt", "Ct", "Dt", "PPQ", "A1024n")}
    r_vpre = [res("v_pre0"), res("v_pre1")]
    r_vown = res("v_ownb")
    r_hbf = res("h_bf")
    r_Spre = res("S_pre")
    r_Sext = res("S_ext")
    r_Hprev = res("Hprev")
    r_Zc = res("Zc")

    TRS = {"Bt": 0, "Ct": 1, "Dt": 2, "PPQ": 3, "A1024n": 4}

    def ptk(buf, k):
        return buf[:, k * 128:(k + 1) * 128]

    def transpose_to(dst_ap, dst_res, src_ap, src_res, slot=None):
        sl = trctr[0] % 2
        trctr[0] += 1
        P.op("tensor", lambda t, s=src_ap, sl=sl: t.matmul(
            pvy[sl][:, 0:128], s, ident_bf[:], start=True, stop=True,
            skip_group_check=True),
            reads=[src_res, r_const], writes=[r_pvy[sl]])
        P.op("scalar", lambda sc, d=dst_ap, sl=sl: sc.copy(
            d, pvy[sl][:, 0:128]), reads=[r_pvy[sl]], writes=[dst_res])

    def emit_head(h):
        hb = h % 2
        A_sb = abcd[hb][:, 0:128]
        B_sb = abcd[hb][:, 128:256]
        C_sb = abcd[hb][:, 256:384]
        D_sb = abcd[hb][:, 384:512]
        myv, r_myv = v_pre[hb], r_vpre[hb]
        myph, r_myph = phSB[hb], r_phSB[hb]
        pagg, r_pagg = paggB[hb], r_paggB[hb]
        myntop, r_myntop = ntop[hb], r_ntop[hb]
        PtB, r_Pt = PtBB[hb], r_PtB[hb]
        PPtB, r_PPt = PPtB1, r_PPt1
        trs, r_trs = trs1, r_trs1

        def trsl(nm):
            o = TRS[nm] * 128
            return trs[:, o:o + 128]

        def chain_doubling(buf, r_buf, base_nat, r_base):
            P.op("scalar", lambda sc, b=base_nat: sc.copy(myntop[:], b),
                 reads=[r_base], writes=[r_myntop])
            P.op("tensor", lambda t, b=base_nat: t.matmul(
                pvy[0][:, 0:128], b, ident_bf[:], start=True, stop=True,
                skip_group_check=True),
                reads=[r_base, r_const], writes=[r_pvy[0]])
            P.op("scalar", lambda sc, buf=buf: sc.copy(
                ptk(buf, 1), pvy[0][:, 0:128]), reads=[r_pvy[0]], writes=[r_buf])
            kk = 1
            while kk < L:
                nn = min(kk, L - kk)
                done = 0
                while done < nn:
                    take = min(4, nn - done)
                    pb = (done // 4) % 2
                    P.op("tensor", lambda t, done=done, take=take, pb=pb,
                         buf=buf:
                         t.matmul(pvy[pb][:, 0:take * 128], myntop[:],
                                  buf[:, (1 + done) * 128:(1 + done + take) * 128],
                                  start=True, stop=True, skip_group_check=True),
                         reads=[r_myntop, r_buf], writes=[r_pvy[pb]])
                    P.op("scalar", lambda sc, kk=kk, done=done, take=take, pb=pb,
                         buf=buf: sc.copy(
                             buf[:, (kk + 1 + done) * 128:(kk + 1 + done + take) * 128],
                             pvy[pb][:, 0:take * 128]),
                         reads=[r_pvy[pb]], writes=[r_buf])
                    done += take
                if kk + nn < L:
                    P.op("tensor", lambda t, buf=buf, nn=nn: t.matmul(
                        pvy[1][:, 0:128], ptk(buf, nn), myntop[:],
                        start=True, stop=True, skip_group_check=True),
                        reads=[r_buf, r_myntop], writes=[r_pvy[1]])
                    P.op("scalar", lambda sc: sc.copy(myntop[:], pvy[1][:, 0:128]),
                         reads=[r_pvy[1]], writes=[r_myntop])
                kk += nn

        for idx, src in ((0, Am_d), (1, Bm_d), (2, Cm_d), (3, Dm_d)):
            P.op("gpsimd", lambda g, idx=idx, src=src: g.dma_start(
                out=abcd[hb][:, idx * 128:(idx + 1) * 128], in_=src[h, :, :]),
                writes=[r_abcd[hb]], dma=r_abcd[hb].name)

        transpose_to(trsl("Bt"), r_trs["Bt"], B_sb, r_abcd[hb])
        transpose_to(trsl("Ct"), r_trs["Ct"], C_sb, r_abcd[hb])
        transpose_to(trsl("Dt"), r_trs["Dt"], D_sb, r_abcd[hb])

        P.op("vector", lambda v: v.tensor_copy(ptk(PtB, 0), ident_bf[:]),
             reads=[r_const], writes=[r_Pt])
        chain_doubling(PtB, r_Pt, A_sb, r_abcd[hb])

        uof = u_own[:, h, :, :].rearrange("p i g -> p (i g)")
        for half in range(2):
            pb = half % 2
            P.op("tensor", lambda t, half=half, pb=pb, uof=uof: t.matmul(
                pvy[pb][:], trsl("Bt"), uof[:, half * 512:(half + 1) * 512],
                start=True, stop=True, skip_group_check=True),
                reads=[r_trs["Bt"], r_uo], writes=[r_pvy[pb]])
            P.op("scalar", lambda sc, half=half, pb=pb: sc.copy(
                v_ownb[:, half * 512:(half + 1) * 512], pvy[pb][:]),
                reads=[r_pvy[pb]], writes=[r_vown])

        upf = u_prec[:, h, :, :].rearrange("p i g -> p (i g)")
        for j in range(6):
            pb = j % 2
            P.op("tensor", lambda t, j=j, pb=pb: t.matmul(
                pvy[pb][:], trsl("Bt"), upf[:, j * 512:(j + 1) * 512],
                start=True, stop=True, skip_group_check=True),
                reads=[r_trs["Bt"], r_up], writes=[r_pvy[pb]])
            P.op("scalar", lambda sc, j=j, pb=pb, myv=myv: sc.copy(
                myv[:, j * 512:(j + 1) * 512], pvy[pb][:]),
                reads=[r_pvy[pb]], writes=[r_myv])

        for i in range(L):
            P.op("tensor", lambda t, i=i: t.matmul(
                pagg[:, 0:NGP], ptk(PtB, L - 1 - i),
                myv[:, i * NGP:(i + 1) * NGP],
                start=(i == 0), stop=(i == L - 1), skip_group_check=True),
                reads=[r_Pt, r_myv], writes=([r_pagg] if i in (0, L - 1) else []))
        P.op("vector", lambda v: v.tensor_mul(S_pre[:], pagg[:, 0:NGP], maskb[:]),
             reads=[r_pagg, r_maskb], writes=[r_Spre])

        first_conv = True
        for k in range(L):
            c0 = k * NGO
            pieces = [(c0, 512), (512, TC)] if c0 < 512 else [(c0, TC)]
            for (a, bnd) in pieces:
                if a >= bnd:
                    continue
                last = (k == L - 1 and bnd == TC)
                P.op("tensor", lambda t, k=k, a=a, bnd=bnd, myph=myph: t.matmul(
                    myph[:, a:bnd], ptk(PtB, k),
                    v_ownb[:, a - k * NGO:bnd - k * NGO],
                    start=(k == 0), stop=False, skip_group_check=True),
                    reads=[r_Pt, r_vown],
                    writes=([r_myph] if (first_conv or last) else []))
                first_conv = False
        P.op("vector", lambda v, myph=myph: v.tensor_copy(
            S_ext[:, 1:L], myph[:, 992:1023]), reads=[r_myph], writes=[r_Sext])

        P.op("tensor", lambda t: t.matmul(pvy[0][:, 0:128], ptk(PtB, L // 2),
             myntop[:], start=True, stop=True, skip_group_check=True),
             reads=[r_Pt, r_myntop], writes=[r_pvy[0]])
        P.op("scalar", lambda sc: sc.copy(trsl("PPQ"), pvy[0][:, 0:128]),
             reads=[r_pvy[0]], writes=[r_trs["PPQ"]])
        P.op("vector", lambda v: v.tensor_copy(ptk(PPtB, 0), ident_bf[:]),
             reads=[r_const], writes=[r_PPt])
        chain_doubling(PPtB, r_PPt, trsl("PPQ"), r_trs["PPQ"])

        for m in range(L):
            P.op("tensor", lambda t, m=m: t.matmul(
                pagg[:, 256:259], ptk(PPtB, m), S_pre[:, 31 - m::L],
                start=(m == 0), stop=(m == L - 1), skip_group_check=True),
                reads=[r_PPt, r_Spre], writes=([r_pagg] if m in (0, L - 1) else []))
        P.op("vector", lambda v: v.tensor_copy(Zc[:, 0:3], pagg[:, 256:259]),
             reads=[r_pagg], writes=[r_Zc])

        P.op("tensor", lambda t: t.matmul(pvy[1][:, 0:128], ptk(PPtB, L // 2),
             myntop[:], start=True, stop=True, skip_group_check=True),
             reads=[r_PPt, r_myntop], writes=[r_pvy[1]])
        P.op("scalar", lambda sc: sc.copy(trsl("A1024n"), pvy[1][:, 0:128]),
             reads=[r_pvy[1]], writes=[r_trs["A1024n"]])
        P.op("tensor", lambda t: t.matmul(pagg[:, 300:428], trsl("A1024n"),
             ptk(PPtB, L), start=True, stop=True, skip_group_check=True),
             reads=[r_trs["A1024n"], r_PPt], writes=[r_pagg])
        P.op("vector", lambda v: v.tensor_copy(trsl("PPQ"), pagg[:, 300:428]),
             reads=[r_pagg], writes=[r_trs["PPQ"]])
        P.op("tensor", lambda t: t.matmul(pagg[:, 448:449], trsl("PPQ"),
             Zc[:, 0:1], start=True, stop=False, skip_group_check=True),
             reads=[r_trs["PPQ"], r_Zc], writes=[r_pagg])
        P.op("tensor", lambda t: t.matmul(pagg[:, 448:449], ptk(PPtB, L),
             Zc[:, 1:2], start=False, stop=False, skip_group_check=True),
             reads=[r_PPt, r_Zc])
        P.op("tensor", lambda t: t.matmul(pagg[:, 448:449], ident_bf[:],
             Zc[:, 2:3], start=False, stop=True, skip_group_check=True),
             reads=[r_const, r_Zc], writes=[r_pagg])
        P.op("vector", lambda v: v.tensor_copy(S_ext[:, 0:1], pagg[:, 448:449]),
             reads=[r_pagg], writes=[r_Sext])

        for m in range(NGO):
            P.op("tensor", lambda t, m=m: t.matmul(
                pagg[:, 256 + m:256 + NGO], ptk(PPtB, m), S_ext[:, 0:NGO - m],
                start=(m == 0), stop=(m == NGO - 1), skip_group_check=True),
                reads=[r_PPt, r_Sext], writes=([r_pagg] if m in (0, NGO - 1) else []))
        P.op("vector", lambda v: v.tensor_copy(Hprev[:], pagg[:, 256:256 + NGO]),
             reads=[r_pagg], writes=[r_Hprev])

        for i in range(L):
            first = (i == 0)
            last = (i == L - 1)
            P.op("tensor", lambda t, i=i, myph=myph: t.matmul(
                myph[:, i * NGO:(i + 1) * NGO], ptk(PtB, i + 1), Hprev[:],
                start=False, stop=last, skip_group_check=True),
                reads=[r_Pt, r_Hprev],
                writes=([r_myph] if (first or last) else []))

        for half in range(2):
            P.op("scalar", lambda sc, half=half, myph=myph: sc.copy(
                h_bf[:, half * 512:(half + 1) * 512],
                myph[:, half * 512:(half + 1) * 512]),
                reads=[r_myph], writes=[r_hbf])
        yv = y_sb[:, h, :].rearrange("p (g i) -> p i g", i=L)
        for half in range(2):
            pb = half % 2
            P.op("tensor", lambda t, half=half, pb=pb: t.matmul(
                pvy[pb][:], trsl("Ct"), h_bf[:, half * 512:(half + 1) * 512],
                start=True, stop=False, skip_group_check=True),
                reads=[r_trs["Ct"], r_hbf], writes=[r_pvy[pb]])
            P.op("tensor", lambda t, half=half, pb=pb, uof=uof: t.matmul(
                pvy[pb][:], trsl("Dt"), uof[:, half * 512:(half + 1) * 512],
                start=False, stop=True, skip_group_check=True),
                reads=[r_trs["Dt"], r_uo], writes=[r_pvy[pb]])
            P.op("scalar", lambda sc, half=half, pb=pb, yv=yv: sc.copy(
                yv[:, half * 16:(half + 1) * 16, :],
                pvy[pb][:].rearrange("p (i g) -> p i g", g=NGO)),
                reads=[r_pvy[pb]], writes=[r_y])

    for h in range(H):
        emit_head(h)

    if tap("y", (128, H, TC), BF16):
        P.op("sync", lambda s: s.dma_start(out=tap_d["y"], in_=y_sb[:]),
             reads=[r_y], dma="tap4")

    if stage == "ssm":
        P.op("gpsimd", lambda g: g.dma_start(
            out=out_d[0:128, 0:512], in_=y_sb[:, 0, 0:512]),
            reads=[r_y], dma="out")
        ssc.close()
        ysc.close()
        usc.close()
        finish(nc, ctx, P, perm)
        return nc, tap_d

    P.gate()
    ssc.close()
    usc.close()

    tsc = Scope(nc)
    mm2 = [tsc.psum(f"mm2_{i}", [128, 512], F32) for i in range(2)]
    sums = tsc.psum("sums", [128, 2, TC], F32)
    trfT = tsc.psum("trfT", [128, 128], F32)
    trbT = tsc.psum("trbT", [128, 128], BF16)
    r_mm2 = [res("mm2_0"), res("mm2_1")]
    r_sums2 = res("sums2")
    r_trfT = res("trfT")
    r_trbT = res("trbT")

    meanF = tsc.sb("meanF", [128, TC], F32)
    invF = tsc.sb("invF", [128, TC], F32)
    x2c = tsc.sb("x2c", [128, TC], F32)
    x2bfc = tsc.sb("x2bfc", [128, TC], BF16)
    wrT = [tsc.sb(f"wrT{i}", [128, 16, 128], BF16) for i in range(3)]
    xr = [tsc.sb(f"xr{i}", [128, TC], F32) for i in range(2)]
    r_meanF, r_invF = res("meanF"), res("invF")
    r_x2c, r_x2bfc = res("x2c"), res("x2bfc")
    r_wrT = [res(f"wrT{i}") for i in range(3)]
    r_xr = [res("xr0"), res("xr1")]
    r_sx2d = res("sx2_d")
    r_sspd = res("ssp_d")

    def stats_full(m, last_m, src_bf_ap, r_src):
        for half in range(2):
            t0 = half * 512
            stats_chunk(sums[:, 0, t0:t0 + 512], sums[:, 1, t0:t0 + 512],
                        r_sums2, src_bf_ap[:, t0:t0 + 512], r_src, 512,
                        first=(m == 0), last=(m == last_m), sq0=0)

    def stats_full_finish():
        for half in range(2):
            t0 = half * 512
            stats_finish(sums[:, 0, t0:t0 + 512], sums[:, 1, t0:t0 + 512],
                         r_sums2, meanF[:, t0:t0 + 512], invF[:, t0:t0 + 512],
                         x2c[:, t0:t0 + 512], r_meanF, r_invF, r_x2c)

    def evac_op(m, pst, prs, ti):
        xrb, r_xrb = xr[m % 2], r_xr[m % 2]
        if ti == 0:
            P.op("sync", lambda s, m=m, d=xrb: s.dma_start(
                out=d[:].rearrange("p (a t) -> p a t", a=8), in_=xo_d[:, :, m, :].rearrange("a p t -> p a t")), reads=[],
                writes=[r_xrb], dma=r_xrb.name)
        t0 = ti * 512
        P.op("vector", lambda v, m=m, t0=t0, pst=pst, xrb=xrb: v.scalar_tensor_tensor(
            x2c[:, t0:t0 + 512], pst[:, 0:512], lncol("outb", m),
            xrb[:, t0:t0 + 512], ALU.add, ALU.add),
            reads=[prs, r_vecs, r_xrb], writes=[r_x2c])
        if ti == 1:
            P.op("scalar", lambda sc: sc.copy(x2bfc[:], x2c[:]),
                 reads=[r_x2c], writes=[r_x2bfc])
            stats_full(m, NCH - 1, x2bfc, r_x2bfc)
            P.op("sync", lambda s, m=m: s.dma_start(out=sx2_d[:, m, :],
                 in_=x2c[:]), reads=[r_x2c], writes=[r_sx2d], dma="sx2")

    mmP, r_mmP = mm2, r_mm2
    proj_sweep(wrT, r_wrT, out_w_d, D, range(NCH), [(0, 512), (512, 512)],
               lambda k, t0, tw: y_sb[:, k, t0:t0 + tw], [r_y], evac_op)
    stats_full_finish()

    if tap("x2", (128, NCH, TC)):
        P.op("sync", lambda s: s.dma_start(out=tap_d["x2"], in_=sx2_d[:]),
             reads=[r_sx2d], dma="tap5")

    P.gate()
    ysc.close()
    t1sc = Scope(nc)
    xt2 = t1sc.sb("xt2", [128, NCH, TC], BF16, side="right")
    r_xt2 = res("xt2")
    for m in range(NCH):
        xrb, r_xrb = xr[m % 2], r_xr[m % 2]
        P.op("sync", lambda s, m=m, d=xrb: s.dma_start(
            out=d[:], in_=sx2_d[:, m, :]), reads=[r_sx2d],
            writes=[r_xrb], dma=r_xrb.name)
        P.op("vector", lambda v, xrb=xrb: v.tensor_sub(
            xrb[:], xrb[:], meanF[:]), reads=[r_meanF], writes=[r_xrb])
        P.op("vector", lambda v, xrb=xrb: v.tensor_mul(
            xrb[:], xrb[:], invF[:]), reads=[r_invF], writes=[r_xrb])
        P.op("scalar", lambda sc, m=m, xrb=xrb: sc.activation(
            xt2[:, m, :], xrb[:], AF.Identity,
            bias=lncol("ln2b", m), scale=lncol("ln2w", m)),
            reads=[r_xrb, r_vecs], writes=[r_xt2])
    if tap("xt2", (128, NCH, TC), BF16):
        P.op("sync", lambda s: s.dma_start(out=tap_d["xt2"], in_=xt2[:]),
             reads=[r_xt2], dma="tap6")

    h_ffn = t1sc.sb("h_ffn", [128, NH, TC], BF16)
    r_hffn = res("h_ffn")

    def evac_f1(m, pst, prs, ti):
        t0 = ti * 512
        P.op("scalar", lambda s, m=m, t0=t0, pst=pst: s.activation(
            h_ffn[:, m, t0:t0 + 512], pst[:, 0:512], AF.Gelu,
            bias=b1s[:, m:m + 1], scale=1.0),
            reads=[prs, r_vecs], writes=[r_hffn])

    proj_sweep(wrT, r_wrT, w1_d, D, range(NH), [(0, 512), (512, 512)],
               lambda k, t0, tw: xt2[:, k, t0:t0 + tw], [r_xt2], evac_f1)

    P.gate()
    t1sc.close_right()
    t2sc = Scope(nc)
    wr64 = [t2sc.sb(f"wr64_{i}", [128, 64, 128], BF16, side="right")
            for i in range(2)]
    r_wr64 = [res("wr64_0"), res("wr64_1")]
    xr2 = [t2sc.sb(f"xr2{i}", [128, TC], F32) for i in range(2)]
    r_xr2 = [res("xr2_0"), res("xr2_1")]

    def evac_f2(m, pst, prs, ti):
        xrb, r_xrb = xr[m % 2], r_xr[m % 2]
        xob, r_xob = xr2[m % 2], r_xr2[m % 2]
        if ti == 0:
            P.op("sync", lambda s, m=m, d=xrb: s.dma_start(
                out=d[:], in_=sx2_d[:, m, :]), reads=[r_sx2d],
                writes=[r_xrb], dma=r_xrb.name)
            P.op("sync", lambda s, m=m, d=xob: s.dma_start(
                out=d[:].rearrange("p (a t) -> p a t", a=8), in_=xo_d[:, :, m, :].rearrange("a p t -> p a t")), reads=[],
                writes=[r_xob], dma=r_xob.name)
        t0 = ti * 512
        P.op("vector", lambda v, m=m, t0=t0, pst=pst, xrb=xrb: v.scalar_tensor_tensor(
            x2c[:, t0:t0 + 512], pst[:, 0:512], b2s[:, m:m + 1],
            xrb[:, t0:t0 + 512], ALU.add, ALU.add),
            reads=[prs, r_vecs, r_xrb], writes=[r_x2c])
        P.op("vector", lambda v, t0=t0, xob=xob: v.tensor_add(
            x2c[:, t0:t0 + 512], x2c[:, t0:t0 + 512], xob[:, t0:t0 + 512]),
            reads=[r_xob], writes=[r_x2c])
        if ti == 1:
            P.op("scalar", lambda sc: sc.copy(x2bfc[:], x2c[:]),
                 reads=[r_x2c], writes=[r_x2bfc])
            stats_full(m, NCH - 1, x2bfc, r_x2bfc)
            P.op("sync", lambda s, m=m: s.dma_start(out=ssp_d[:, m, :],
                 in_=x2c[:]), reads=[r_x2c], writes=[r_sspd], dma="ssp")

    proj_sweep(wr64, r_wr64, w2_d, HID, range(NCH), [(0, 512), (512, 512)],
               lambda k, t0, tw: h_ffn[:, k, t0:t0 + tw], [r_hffn], evac_f2)
    stats_full_finish()

    P.gate()
    t2sc.close()
    t1sc.close()
    t3sc = Scope(nc)
    xA = t3sc.sb("xA", [128, NCH, TC], F32)
    he = t3sc.sb("he", [128, NSH, TC], BF16)
    Cb = t3sc.sb("Cb", [128, TC], BF16)
    rsc = t3sc.sb("rsc", [128, 8], F32)
    rs2 = t3sc.sb("rs2", [128, 12], F32)
    combbf = t3sc.sb("combbf", [128, 8], BF16)
    combT = t3sc.sb("combT", [8, TC], BF16)
    selE = t3sc.sb("selE", [8, 8 * 128], BF16)
    wrb = t3sc.sb("wrb", [128, NCH, 8], BF16)
    tsub = Scope(nc)
    t1b = tsub.sb("t1b", [128, TC], BF16)
    t2b = tsub.sb("t2b", [128, TC], BF16)
    xAbf = t3sc.sb("xAbf", [128, NCH, TC], BF16, side="right")
    wrM = [t3sc.sb(f"wrM{i}", [128, 22, 128], BF16, side="right")
           for i in range(3)]
    r_xAp = [res("xA_ev"), res("xA_od")]
    r_he = res("he")
    r_Cb = res("Cb")
    r_t1b, r_t2b = res("t1b"), res("t2b")
    r_rsc, r_rs2 = res("rsc"), res("rs2")
    r_combbf, r_combT = res("combbf"), res("combT")
    r_selE = res("selE")
    r_wrb = res("wrb")
    r_xAbf = res("xAbf")
    r_wrM = [res(f"wrM{i}") for i in range(3)]

    P.op("gpsimd", lambda g: g.memset(selE[:], 0.0), writes=[r_selE])
    P.op("gpsimd", lambda g: g.affine_select(
        out=selE[:].rearrange("p (e i) -> p e i", i=128),
        in_=selE[:].rearrange("p (e i) -> p e i", i=128),
        compare_op=ALU.not_equal, fill=1.0, base=0,
        pattern=[[-1, E], [0, 128]], channel_multiplier=1), writes=[r_selE])
    P.op("gpsimd", lambda g: g.dma_start(
        out=wrb[:], in_=wr_d.rearrange("(k p) e -> p k e", p=128)),
        writes=[r_wrb], dma="wrb")

    for m in range(NCH):
        xrb, r_xrb = xr[m % 2], r_xr[m % 2]
        P.op("sync", lambda s, m=m, d=xrb: s.dma_start(
            out=d[:], in_=ssp_d[:, m, :]), reads=[r_sspd],
            writes=[r_xrb], dma=r_xrb.name)
        P.op("vector", lambda v, xrb=xrb: v.tensor_sub(
            xrb[:], xrb[:], meanF[:]), reads=[r_meanF], writes=[r_xrb])
        P.op("vector", lambda v, m=m, xrb=xrb: v.tensor_mul(
            xA[:, m, :], xrb[:], invF[:]), reads=[r_xrb, r_invF],
            writes=[r_xAp[m % 2]])
        P.op("scalar", lambda sc, m=m: sc.copy(xAbf[:, m, :], xA[:, m, :]),
             reads=[r_xAp[m % 2]], writes=[r_xAbf])
    if tap("xA", (128, NCH, TC)):
        P.op("sync", lambda s: s.dma_start(out=tap_d["xA"], in_=xA[:]),
             reads=[r_xAp[0], r_xAp[1]], dma="tap7")

    for tt in range(8):
        tt0 = tt * 128
        for k in range(NCH):
            P.op("tensor", lambda t, k=k, tt0=tt0: t.matmul(
                trfT[:, 0:8], xAbf[:, k, tt0:tt0 + 128], wrb[:, k, :],
                start=(k == 0), stop=(k == NCH - 1), skip_group_check=True),
                reads=[r_xAbf, r_wrb],
                writes=([r_trfT] if k in (0, NCH - 1) else []))
        P.op("vector", lambda v: v.tensor_copy(rsc[:], trfT[:, 0:8]),
             reads=[r_trfT], writes=[r_rsc])
        P.op("vector", lambda v: v.reduce_max(rs2[:, 0:1], rsc[:], axis=AX.X,
             negate=True), reads=[r_rsc], writes=[r_rs2])
        P.op("scalar", lambda s: s.activation(rsc[:], rsc[:], AF.Exp,
             bias=rs2[:, 0:1], scale=1.0), reads=[r_rs2], writes=[r_rsc])
        P.op("vector", lambda v: v.reduce_sum(rs2[:, 1:2], rsc[:], axis=AX.X),
             reads=[r_rsc], writes=[r_rs2])
        P.op("vector", lambda v: v.reciprocal(rs2[:, 2:3], rs2[:, 1:2]),
             writes=[r_rs2])
        P.op("vector", lambda v: v.tensor_scalar_mul(rsc[:], rsc[:],
             rs2[:, 2:3]), reads=[r_rs2], writes=[r_rsc])
        P.op("vector", lambda v: v.reduce_max(rs2[:, 3:4], rsc[:], axis=AX.X),
             reads=[r_rsc], writes=[r_rs2])
        P.op("vector", lambda v: v.tensor_single_scalar(rs2[:, 4:12], rsc[:],
             rs2[:, 3:4], ALU.is_ge), reads=[r_rsc], writes=[r_rs2])
        P.op("vector", lambda v: v.scalar_tensor_tensor(rs2[:, 4:12],
             rs2[:, 4:12], -1e30, rsc[:], ALU.mult, ALU.add),
             reads=[r_rsc], writes=[r_rs2])
        P.op("vector", lambda v: v.reduce_max(rs2[:, 3:4], rs2[:, 4:12],
             axis=AX.X), writes=[r_rs2])
        P.op("vector", lambda v: v.tensor_single_scalar(rs2[:, 4:12], rsc[:],
             rs2[:, 3:4], ALU.is_ge), reads=[r_rsc], writes=[r_rs2])
        P.op("vector", lambda v: v.tensor_mul(combbf[:], rsc[:], rs2[:, 4:12]),
             reads=[r_rsc, r_rs2], writes=[r_combbf])
        P.op("tensor", lambda t: t.transpose(
            trbT[0:8, :], combbf[:], ident_bf[:]),
            reads=[r_combbf, r_const], writes=[r_trbT])
        P.op("vector", lambda v, tt0=tt0: v.tensor_copy(
            combT[:, tt0:tt0 + 128], trbT[0:8, :]),
            reads=[r_trbT], writes=[r_combT])

    def gateup_sweep(wg_hbm, wu_hbm, n_m, kdim, cb_scale, dst, r_dst):
        nk = kdim // 128
        wl = []
        for m in range(n_m):
            wl.append(wg_hbm[m])
            wl.append(wu_hbm[m])
        ws = WStream(wrM, r_wrM, wl, ahead=len(wrM) - 1)
        for m in range(n_m):
            wg, r_wg = ws.get(2 * m)
            wu, r_wu = ws.get(2 * m + 1)
            for ti in range(2):
                t0 = ti * 512
                pg, r_pg = mm2[0], r_mm2[0]
                pu, r_pu = mm2[1], r_mm2[1]
                for k in range(nk):
                    P.op("tensor", lambda t, k=k, t0=t0, wg=wg: t.matmul(
                        pg[:, 0:512], wg[:, k, :], xAbf[:, k, t0:t0 + 512],
                        start=(k == 0), stop=(k == nk - 1),
                        skip_group_check=True),
                        reads=[r_wg, r_xAbf],
                        writes=([r_pg] if k in (0, nk - 1) else []))
                P.op("scalar", lambda s, t0=t0: s.activation(
                    t1b[:, t0:t0 + 512], pg[:, 0:512], AF.Silu),
                    reads=[r_pg], writes=[r_t1b])
                for k in range(nk):
                    P.op("tensor", lambda t, k=k, t0=t0, wu=wu: t.matmul(
                        pu[:, 0:512], wu[:, k, :], xAbf[:, k, t0:t0 + 512],
                        start=(k == 0), stop=(k == nk - 1),
                        skip_group_check=True),
                        reads=[r_wu, r_xAbf],
                        writes=([r_pu] if k in (0, nk - 1) else []))
                if cb_scale:
                    P.op("vector", lambda v, t0=t0: v.tensor_mul(
                        t2b[:, t0:t0 + 512], pu[:, 0:512], Cb[:, t0:t0 + 512]),
                        reads=[r_pu, r_Cb], writes=[r_t2b])
                else:
                    P.op("vector", lambda v, t0=t0: v.tensor_copy(
                        t2b[:, t0:t0 + 512], pu[:, 0:512]),
                        reads=[r_pu], writes=[r_t2b])
                P.op("vector", lambda v, m=m, t0=t0, dst=dst: v.tensor_mul(
                    dst[:, m, t0:t0 + 512], t1b[:, t0:t0 + 512],
                    t2b[:, t0:t0 + 512]),
                    reads=[r_t1b, r_t2b], writes=[r_dst])

    def down_sweep(wd_hbm, nk, src, r_src):
        ws = WStream(wrM, r_wrM, [wd_hbm[m] for m in range(NCH)])
        for m in range(NCH):
            wd, r_wd = ws.get(m)
            for ti in range(2):
                t0 = ti * 512
                pb = ti % 2
                pst, prs = mm2[pb], r_mm2[pb]
                for k in range(nk):
                    P.op("tensor", lambda t, k=k, t0=t0, wd=wd, src=src,
                         pst=pst: t.matmul(
                        pst[:, 0:512], wd[:, k, :], src[:, k, t0:t0 + 512],
                        start=(k == 0), stop=(k == nk - 1),
                        skip_group_check=True),
                        reads=[r_wd, r_src],
                        writes=([prs] if k in (0, nk - 1) else []))
                P.op("vector", lambda v, m=m, t0=t0, pst=pst: v.tensor_add(
                    xA[:, m, t0:t0 + 512], xA[:, m, t0:t0 + 512],
                    pst[:, 0:512]), reads=[prs], writes=[r_xAp[m % 2]])

    gateup_sweep(wsg_d, wsu_d, NSH, D, False, he, r_he)
    down_sweep(wsd_d, NSH, he, r_he)

    for e in range(E):
        for half in range(2):
            t0 = half * 512
            pb = half % 2
            P.op("tensor", lambda t, e=e, t0=t0, pb=pb: t.matmul(
                mm2[pb][:, 0:512], selE[:, e * 128:(e + 1) * 128],
                combT[:, t0:t0 + 512], start=True, stop=True,
                skip_group_check=True),
                reads=[r_selE, r_combT], writes=[r_mm2[pb]])
            P.op("vector", lambda v, t0=t0, pb=pb: v.tensor_copy(
                Cb[:, t0:t0 + 512], mm2[pb][:, 0:512]),
                reads=[r_mm2[pb]], writes=[r_Cb])
        gateup_sweep(weg_d[e], weu_d[e], NEH, D, True, he, r_he)
        down_sweep(wed_d[e], NEH, he, r_he)

    tsub.close()
    for m in range(NCH):
        P.op("scalar", lambda sc, m=m: sc.copy(x2bfc[:], xA[:, m, :]),
             reads=[r_xAp[m % 2]], writes=[r_x2bfc])
        stats_full(m, NCH - 1, x2bfc, r_x2bfc)
    stats_full_finish()

    otm = [t3sc.sb(f"otm{i}", [128, 512], F32) for i in range(2)]
    r_otm = [res("otm0"), res("otm1")]
    for qd in range(4):
        for cc in range(4):
            m = qd * 4 + cc
            P.op("vector", lambda v, m=m: v.tensor_sub(
                xA[:, m, :], xA[:, m, :], meanF[:]), reads=[r_meanF],
                writes=[r_xAp[m % 2]])
            P.op("vector", lambda v, m=m: v.tensor_mul(
                xA[:, m, :], xA[:, m, :], invF[:]), reads=[r_invF],
                writes=[r_xAp[m % 2]])
        for blk in range(8):
            b0 = blk * 128
            ob = (qd * 8 + blk) % 2
            for cc in range(4):
                c = qd * 4 + cc
                sl = c % 2
                P.op("tensor", lambda t, c=c, b0=b0, sl=sl: t.transpose(
                    mm2[sl][:, 0:128], xA[:, c, b0:b0 + 128], ident_f[:]),
                    reads=[r_xAp[c % 2], r_const], writes=[r_mm2[sl]])
                if cc % 2 == 0:
                    P.op("scalar", lambda sc, cc=cc, sl=sl, ob=ob: sc.copy(
                        otm[ob][:, cc * 128:(cc + 1) * 128], mm2[sl][:, 0:128]),
                        reads=[r_mm2[sl]], writes=[r_otm[ob]])
                else:
                    P.op("vector", lambda v, cc=cc, sl=sl, ob=ob: v.tensor_copy(
                        otm[ob][:, cc * 128:(cc + 1) * 128], mm2[sl][:, 0:128]),
                        reads=[r_mm2[sl]], writes=[r_otm[ob]])
            P.op("sync", lambda s, b0=b0, qd=qd, ob=ob: s.dma_start(
                out=out_d[b0:b0 + 128, qd * 512:(qd + 1) * 512],
                in_=otm[ob][:]), reads=[r_otm[ob]], dma="out")

    t3sc.close()
    tsc.close()
    finish(nc, ctx, P, perm)
    return nc, tap_d


def finish(nc, ctx, P, perm):
    P.ins["sync"].append([lambda s: s.nop(),
                          [("D", k, c) for k, c in P.dmacnt.items()],
                          False, None])
    P.finalize(nc, ctx)
    with nc.Block() as block:
        @block.tensor
        def _(t):
            P.run_engine("tensor", t)

        @block.vector
        def _(v):
            P.run_engine("vector", v)

        @block.scalar
        def _(s):
            P.run_engine("scalar", s)

        @block.sync
        def _(s):
            P.run_engine("sync", s)

        @block.gpsimd
        def _(g):
            P.run_engine("gpsimd", g)
    perm.close()
    ctx.close()


def make_in_maps(inputs):
    x = np.ascontiguousarray(inputs["L0_attn_x"])

    def vecN(v, n):
        return np.ascontiguousarray(np.asarray(v, np.float32).reshape(n, 128).T)

    def wcolmajor(w):
        w = np.asarray(w, np.float32)
        K, M = w.shape
        return np.ascontiguousarray(
            w.reshape(K // 128, 128, M // 128, 128).transpose(2, 1, 0, 3))

    def wcolmajor_e(w):
        return np.ascontiguousarray(
            np.stack([wcolmajor(w[e]) for e in range(w.shape[0])]))

    shared = {
        "ln1w": vecN(inputs["L0_attn_ln1_w"], 16), "ln1b": vecN(inputs["L0_attn_ln1_b"], 16),
        "ln2w": vecN(inputs["L0_attn_ln2_w"], 16), "ln2b": vecN(inputs["L0_attn_ln2_b"], 16),
        "inb": vecN(inputs["L0_attn_in_b"], 16), "outb": vecN(inputs["L0_attn_out_b"], 16),
        "b2v": vecN(inputs["L0_attn_b2"], 16), "b1v": vecN(inputs["L0_attn_b1"], 64),
        "in_w": wcolmajor(inputs["L0_attn_in_w"]),
        "out_w": wcolmajor(inputs["L0_attn_out_w"]),
        "w1": wcolmajor(inputs["L0_attn_w1"]),
        "w2": wcolmajor(inputs["L0_attn_w2"]),
        "wsg": wcolmajor(inputs["L0_ffn_Ws_gate"]),
        "wsu": wcolmajor(inputs["L0_ffn_Ws_up"]),
        "wsd": wcolmajor(inputs["L0_ffn_Ws_down"]),
        "weg": wcolmajor_e(inputs["L0_ffn_We_gate"]),
        "weu": wcolmajor_e(inputs["L0_ffn_We_up"]),
        "wed": wcolmajor_e(inputs["L0_ffn_We_down"]),
        "wr": np.ascontiguousarray(inputs["L0_ffn_W_router"]),
        "Am": np.ascontiguousarray(inputs["L0_attn_A"]),
        "Bm": np.ascontiguousarray(inputs["L0_attn_Bm"]),
        "Cm": np.ascontiguousarray(inputs["L0_attn_Cm"]),
        "Dm": np.ascontiguousarray(inputs["L0_attn_Dm"]),
    }
    def fmaj(a):
        nt = a.shape[0] // 128
        return np.ascontiguousarray(
            a.reshape(nt, 128, NCH, 128).transpose(0, 3, 2, 1))

    in_maps = []
    for c in range(8):
        b, q = c // 4, c % 4
        t0 = q * TC
        xo = fmaj(x[b, t0:t0 + TC])
        xpad = np.zeros((TP, D), dtype=np.float32)
        lo = t0 - TP
        v0 = max(0, -lo)
        if t0 > 0:
            xpad[v0:] = x[b, max(0, lo):t0]
        mask = np.zeros((1, NGP), dtype=np.float32)
        mask[0, v0 // L:] = 1.0
        m = dict(shared)
        m.update({"xo": xo, "xp": fmaj(xpad), "maskp": mask})
        in_maps.append(m)
    return in_maps


def kernel(**inputs):
    nc, _ = build_nc(stage="full")
    in_maps = make_in_maps(inputs)
    res = run_bass_kernel_spmd(nc, in_maps, core_ids=list(range(8)))
    out = np.zeros((B, T, D), dtype=np.float32)
    for c in range(8):
        b, q = c // 4, c % 4
        out[b, q * TC:(q + 1) * TC] = res.results[c]["out"]
    return out


# version 17 (final)
# speedup vs baseline: 1.0761x; 1.0761x over previous
import dataclasses
import numpy as np
from contextlib import ExitStack

import concourse.bass as bass
import concourse.mybir as mybir
from concourse.bass_utils import run_bass_kernel_spmd

F32 = mybir.dt.float32
BF16 = mybir.dt.bfloat16
AF = mybir.ActivationFunctionType
ALU = mybir.AluOpType
AX = mybir.AxisListType

D = 2048
NCH = 16
H = 16
B = 2
T = 4096
TC = 1024
TP = 3072
L = 32
NGO = TC // L
NGP = TP // L
HID = 8192
NH = 64
DS = 2816
NSH = 22
DE = 1408
NEH = 11
E = 8
EPS = 1e-5

def bcast(ap, dim, count):
    l = [list(x) for x in ap.ap]
    l.insert(dim, [0, count])
    return dataclasses.replace(ap, ap=l)


COMPUTE = ("tensor", "vector", "scalar")
DMAQ = ("sync", "gpsimd")
ENGINES = COMPUTE + DMAQ


class Res:
    __slots__ = ("w", "rs", "name")

    def __init__(self, name=""):
        self.w = None
        self.rs = []
        self.name = name


class Plan:

    def __init__(self):
        self.ins = {e: [] for e in ENGINES}
        self.floor = []
        self.dmacnt = {}
        self.dmakey = {}
        self.dram_keys = set()
        self.drain_gates = False

    def op(self, eng, builder, reads=(), writes=(), extra_waits=(), dma=None):
        deps = set(self.floor)
        for r in reads:
            if r.w is not None:
                deps.add(r.w)
        for wres in writes:
            if wres.w is not None:
                deps.add(wres.w)
            for t in wres.rs:
                deps.add(t)
        for t in extra_waits:
            if t is not None:
                deps.add(t)
        idx = len(self.ins[eng])
        tok = (eng, idx)
        waits = []
        seen_keys = {}
        for (e2, i2) in deps:
            key = self.dmakey.get((e2, i2))
            if key is not None:
                seen_keys[key] = self.dmacnt[key]
            else:
                waits.append(("C", e2, i2))
        for key, cnt in seen_keys.items():
            waits.append(("D", key, cnt))
        if dma is not None:
            self.dmacnt[dma] = self.dmacnt.get(dma, 0) + 1
            self.dmakey[tok] = dma
        self.ins[eng].append([builder, waits, dma is not None, dma])
        for r in reads:
            r.rs.append(tok)
        for wres in writes:
            wres.w = tok
            wres.rs = []
        return tok

    def gate(self):
        def last(e):
            return (e, len(self.ins[e]) - 1) if self.ins[e] else None
        tv = self.op("vector", lambda v: v.engine_nop(),
                     extra_waits=[last("vector")])
        ts_ = self.op("scalar", lambda s: s.nop(), extra_waits=[last("scalar")])
        tg = self.op("gpsimd", lambda g: g.nop(), extra_waits=[last("gpsimd")])
        tt = self.op("tensor", lambda t: t.nop(),
                     extra_waits=[tv, ts_, tg, last("tensor"), last("sync")])
        self.ins["tensor"][tt[1]][1].extend(
            ("D", k, c) for k, c in self.dmacnt.items()
            if self.drain_gates or k not in self.dram_keys)
        self.floor = [tt]
        return tt

    def finalize(self, nc, ctx):
        csem = {e: ctx.enter_context(nc.semaphore("c_" + e)) for e in ENGINES}
        dsem = {k: ctx.enter_context(nc.semaphore("k_" + k)) for k in self.dmacnt}
        for e in ENGINES:
            for ent in self.ins[e]:
                for w in ent[1]:
                    if w[0] == "C":
                        self.ins[w[1]][w[2]][2] = True
        val = {}
        for e in ENGINES:
            cc = 0
            for i, ent in enumerate(self.ins[e]):
                if ent[2] and ent[3] is None:
                    cc += 1
                val[(e, i)] = cc
        out = {e: [] for e in ENGINES}
        for e in ENGINES:
            hw = {}
            for builder, waits, flagged, key in self.ins[e]:
                byk = {}
                for w in waits:
                    if w[0] == "C":
                        k2 = ("C", w[1])
                        v = val[(w[1], w[2])]
                    else:
                        k2 = ("D", w[1])
                        v = w[2] * 16
                    if v > byk.get(k2, 0):
                        byk[k2] = v
                resolved = []
                for k2, v in byk.items():
                    if v > hw.get(k2, 0):
                        hw[k2] = v
                        sem = csem[k2[1]] if k2[0] == "C" else dsem[k2[1]]
                        resolved.append((sem, v))
                out[e].append((builder, resolved, flagged, key))
        self.final = out
        self.csem = csem
        self.dsem = dsem

    def run_engine(self, eng, engine):
        for builder, waits, flagged, key in self.final[eng]:
            for s, v in waits:
                engine.wait_ge(s, v)
            ins = builder(engine)
            if key is not None:
                ins.then_inc(self.dsem[key], 16)
            elif flagged:
                ins.then_inc(self.csem[eng], 1)


class Scope:

    def __init__(self, nc):
        self.nc = nc
        self.cms = []

    def sb(self, name, shape, dt, side=None):
        cm = self.nc.sbuf_tensor(name, list(shape), dt, side=side)
        h = cm.__enter__()
        self.cms.append((cm, side))
        return h

    def psum(self, name, shape, dt=F32):
        cm = self.nc.psum_tensor(name, list(shape), dt)
        h = cm.__enter__()
        self.cms.append((cm, "psum"))
        return h

    def close_right(self):
        keep = []
        for cm, side in reversed(self.cms):
            if side == "right":
                cm.__exit__(None, None, None)
            else:
                keep.append((cm, side))
        self.cms = list(reversed(keep))

    def close(self):
        for cm, side in reversed(self.cms):
            cm.__exit__(None, None, None)
        self.cms = []


def build_nc(stage="full", taps=()):
    taps = set(taps)
    nc = bass.Bass("TRN2", target_bir_lowering=False, debug=False)

    d_in = {}

    def din(name, shape):
        d_in[name] = nc.dram_tensor(name, list(shape), F32, kind="ExternalInput").ap()
        return d_in[name]

    xo_d = din("xo", (TC // 128, 128, NCH, 128))
    xp_d = din("xp", (TP // 128, 128, NCH, 128))
    din("maskp", (1, NGP))
    for nm in ("ln1w", "ln1b", "ln2w", "ln2b", "inb", "outb", "b2v"):
        din(nm, (128, 16))
    din("b1v", (128, 64))
    in_w_d = din("in_w", (NCH, 128, NCH, 128)); out_w_d = din("out_w", (NCH, 128, NCH, 128))
    w1_d = din("w1", (NH, 128, NCH, 128)); w2_d = din("w2", (NCH, 128, NH, 128))
    wsg_d = din("wsg", (NSH, 128, NCH, 128)); wsu_d = din("wsu", (NSH, 128, NCH, 128))
    wsd_d = din("wsd", (NCH, 128, NSH, 128))
    weg_d = din("weg", (E, NEH, 128, NCH, 128)); weu_d = din("weu", (E, NEH, 128, NCH, 128))
    wed_d = din("wed", (E, NCH, 128, NEH, 128))
    wr_d = din("wr", (D, E))
    Am_d = din("Am", (H, 128, 128)); Bm_d = din("Bm", (H, 128, 128))
    Cm_d = din("Cm", (H, 128, 128)); Dm_d = din("Dm", (H, 128, 128))

    out_d = nc.dram_tensor("out", [TC, D], F32, kind="ExternalOutput").ap()
    tap_d = {}

    def tap(name, shape, dtype=F32):
        if name in taps:
            tap_d[name] = nc.dram_tensor("tap_" + name, list(shape), dtype,
                                         kind="ExternalOutput").ap()
        return tap_d.get(name)

    sx2_d = nc.dram_tensor("sx2", [128, NCH, TC], F32, kind="Internal").ap()
    ssp_d = nc.dram_tensor("ssp", [128, NCH, TC], F32, kind="Internal").ap()

    ctx = ExitStack()
    P = Plan()
    P.drain_gates = bool(taps)
    P.dram_keys = {"sx2", "ssp", "out", "tap1", "tap2", "tap3", "tap4",
                   "tap5", "tap6", "tap7"}
    perm = Scope(nc)

    R = {}

    def res(name):
        if name not in R:
            R[name] = Res(name)
        return R[name]

    ones_bf = perm.sb("ones_bf", [128, 128], BF16)
    ident_bf = perm.sb("ident_bf", [128, 128], BF16)
    ident_f = perm.sb("ident_f", [128, 128], F32)
    lnv = perm.sb("lnv", [128, 16 * 6], F32)
    b1s = perm.sb("b1s", [128, 64], F32)
    b2s = perm.sb("b2s", [128, 16], F32)
    maskb = perm.sb("maskb", [128, NGP], F32)
    mask1 = perm.sb("mask1", [1, NGP], F32)
    sqbf = perm.sb("sqbf", [128, 512], BF16)
    r_sqbf = res("sqbf")

    LVOFF = {"ln1w": 0, "ln1b": 16, "ln2w": 32, "ln2b": 48, "inb": 64, "outb": 80}

    def lncol(nm, m):
        return lnv[:, LVOFF[nm] + m:LVOFF[nm] + m + 1]

    r_const = res("const")
    P.op("gpsimd", lambda g: g.memset(ones_bf[:], 1.0), writes=[r_const])
    P.op("gpsimd", lambda g: g.memset(ident_bf[:], 0.0), writes=[r_const])
    P.op("gpsimd", lambda g: g.affine_select(out=ident_bf[:], in_=ident_bf[:],
         compare_op=ALU.not_equal, fill=1.0, base=0, pattern=[[-1, 128]],
         channel_multiplier=1), writes=[r_const])
    P.op("gpsimd", lambda g: g.memset(ident_f[:], 0.0), writes=[r_const])
    P.op("gpsimd", lambda g: g.affine_select(out=ident_f[:], in_=ident_f[:],
         compare_op=ALU.not_equal, fill=1.0, base=0, pattern=[[-1, 128]],
         channel_multiplier=1), writes=[r_const])

    r_vecs = res("lnvecs")
    for nm, dst in (("ln1w", lnv[:, 0:16]), ("ln1b", lnv[:, 16:32]),
                    ("ln2w", lnv[:, 32:48]), ("ln2b", lnv[:, 48:64]),
                    ("inb", lnv[:, 64:80]), ("outb", lnv[:, 80:96]),
                    ("b1v", b1s[:]), ("b2v", b2s[:]), ("maskp", mask1[:])):
        P.op("sync", lambda s, d=dst, src=d_in[nm]: s.dma_start(out=d, in_=src),
             writes=[r_vecs], dma="vecs")

    usc = Scope(nc)
    u_own = usc.sb("u_own", [128, H, L, NGO], BF16)
    u_prec = usc.sb("u_prec", [128, H, L, NGP], BF16)

    fsc = Scope(nc)
    mmP = [fsc.psum(f"mm{i}", [128, 512], F32) for i in range(2)]
    stP_s = fsc.psum("stP_s", [128, 128], F32)
    stP_q = fsc.psum("stP_q", [128, 128], F32)
    r_mmP = [res("mm0"), res("mm1")]
    r_stP = res("stP")

    r_maskb = res("maskb")
    P.op("vector", lambda v: v.tensor_copy(sqbf[0:1, 0:NGP], mask1[:]),
         reads=[r_vecs], writes=[r_maskb, r_sqbf])
    P.op("tensor", lambda t: t.matmul(stP_s[:, 0:NGP], ones_bf[0:1, :], sqbf[0:1, 0:NGP],
         start=True, stop=True, skip_group_check=True),
         reads=[r_maskb, r_const, r_sqbf], writes=[r_stP])
    P.op("vector", lambda v: v.tensor_copy(maskb[:], stP_s[:, 0:NGP]),
         reads=[r_stP], writes=[r_maskb])

    wr16 = [fsc.sb(f"wr16_{i}", [128, 16, 128], BF16) for i in range(4)]
    r_wr16 = [res(f"wr16_{i}") for i in range(4)]
    class WStream:

        def __init__(self, rbufs, rres, w_list, ahead=None):
            self.bufs, self.res, self.w = rbufs, rres, list(w_list)
            self.R = len(rbufs)
            self.ahead = self.R if ahead is None else ahead
            self.issued = 0

        def _issue(self):
            i = self.issued
            b, r = self.bufs[i % self.R], self.res[i % self.R]
            ap = self.w[i]
            nk = ap.shape[1]
            P.op("gpsimd", lambda g, d=b, ap=ap, nk=nk: g.dma_start(
                out=d[:, 0:nk, :], in_=ap), writes=[r], dma=r.name)
            self.issued += 1

        def get(self, i):
            while self.issued < min(i + self.ahead, len(self.w)):
                self._issue()
            return self.bufs[i % self.R], self.res[i % self.R]

    def stream_wcol(rbufs, rres, w_hbm, nk, m):
        ws = WStream(rbufs, rres, [w_hbm[m]])
        return ws.get(0)

    def proj_sweep(rbufs, rres, w_hbm, kdim, m_list, tiles, mov_fn, mov_res,
                   evac_fn):
        nk = kdim // 128
        ml = list(m_list)
        ws = WStream(rbufs, rres, [w_hbm[m] for m in ml])
        for mi, m in enumerate(ml):
            wsb, wres = ws.get(mi)
            for ti, (t0, tw) in enumerate(tiles):
                pb = (mi * len(tiles) + ti) % len(mmP)
                pst, prs = mmP[pb], r_mmP[pb]
                mres = mov_res(t0) if callable(mov_res) else mov_res
                for k in range(nk):
                    P.op("tensor", lambda t, k=k, pst=pst, t0=t0, tw=tw, wsb=wsb:
                         t.matmul(pst[:, 0:tw], wsb[:, k, :], mov_fn(k, t0, tw),
                                  start=(k == 0), stop=(k == nk - 1),
                                  skip_group_check=True),
                         reads=[wres] + mres,
                         writes=([prs] if k in (0, nk - 1) else []))
                evac_fn(m, pst, prs, ti)

    def stats_chunk(ps_sum, ps_sq, r_ps, xbf_ap, r_x, tw, first, last, sq0=0):
        P.op("scalar", lambda sc: sc.activation(sqbf[:, sq0:sq0 + tw], xbf_ap,
             AF.Square), reads=[r_x], writes=[r_sqbf])
        P.op("tensor", lambda t: t.matmul(ps_sum, ones_bf[:], xbf_ap,
             start=first, stop=last, skip_group_check=True),
             reads=[r_x, r_const], writes=([r_ps] if first else []))
        P.op("tensor", lambda t: t.matmul(ps_sq, ones_bf[:], sqbf[:, sq0:sq0 + tw],
             start=first, stop=last, skip_group_check=True),
             reads=[r_sqbf, r_const], writes=([r_ps] if last else []))

    def stats_finish(ps_sum, ps_sq, r_ps, mean_ap, inv_ap, t1_ap,
                     r_mean, r_inv, r_t1):
        P.op("vector", lambda v: v.tensor_scalar_mul(mean_ap, ps_sum, 1.0 / D),
             reads=[r_ps], writes=[r_mean])
        P.op("vector", lambda v: v.tensor_scalar_mul(t1_ap, ps_sq, 1.0 / D),
             reads=[r_ps], writes=[r_t1])
        P.op("vector", lambda v: v.tensor_mul(inv_ap, mean_ap, mean_ap),
             reads=[r_mean], writes=[r_inv])
        P.op("vector", lambda v: v.tensor_sub(t1_ap, t1_ap, inv_ap),
             reads=[r_inv], writes=[r_t1])
        P.op("vector", lambda v: v.tensor_scalar_add(t1_ap, t1_ap, EPS),
             writes=[r_t1])
        P.op("vector", lambda v: v.reciprocal(inv_ap, t1_ap),
             reads=[r_t1], writes=[r_inv])
        P.op("scalar", lambda s: s.activation(inv_ap, inv_ap, AF.Sqrt),
             writes=[r_inv])

    xt_sweep = fsc.sb("xt_sweep", [128, NCH, TC], BF16, side="right")
    xfm_bf = [fsc.sb(f"xfm_bf{i}", [128, NCH, 128], BF16, side="right")
              for i in range(3)]
    xfm_f = fsc.sb("xfm_f", [128, NCH, 128], F32, side="right")
    mean1 = fsc.sb("mean1", [128, 2, 128], F32, side="right")
    inv1 = fsc.sb("inv1", [128, 2, 128], F32, side="right")
    t1s = fsc.sb("t1s", [128, 128], F32, side="right")
    r_xt_sweep = [res("xt_sweep0"), res("xt_sweep1")]
    r_xtmb = [res("xtmb0"), res("xtmb1")]
    r_xfmb = [res(f"xfmb{i}") for i in range(3)]
    r_xtmf = res("xtmf")
    r_xfmf = res("xfmf")
    r_mean1 = [res("mean1a"), res("mean1b")]
    r_inv1 = [res("inv1a"), res("inv1b")]
    r_t1s = res("t1s")

    r_uo = res("u_own")
    r_up = res("u_prec")

    fr_count = [0]

    def frontend_tile(kind, src_ap, dst_xt, r_dst):
        i = fr_count[0] % 3
        fr_count[0] += 1
        if kind == "prec":
            xbf, r_xbf = xfm_bf[i], r_xfmb[i]
            P.op("gpsimd", lambda g, s=src_ap, d=xbf: g.dma_start(
                out=d[:], in_=s), writes=[r_xbf], dma=r_xbf.name)
        else:
            P.op("sync", lambda s, sa=src_ap: s.dma_start(
                out=xfm_f[:], in_=sa), writes=[r_xfmf], dma="xfmf")
            xbf, r_xbf = xfm_bf[i], r_xfmb[i]
            P.op("gpsimd", lambda g, s=src_ap, d=xbf: g.dma_start(
                out=d[:], in_=s), writes=[r_xbf], dma=r_xbf.name)
        pr = i % 2
        for c in range(NCH):
            stats_chunk(stP_s[:], stP_q[:], r_stP,
                        xbf[:, c, :], r_xbf, 128,
                        first=(c == 0), last=(c == NCH - 1))
        stats_finish(stP_s[:], stP_q[:], r_stP,
                     mean1[:, pr, :], inv1[:, pr, :], t1s[:],
                     r_mean1[pr], r_inv1[pr], r_t1s)
        src_t, r_srct = (xbf, r_xbf) if kind == "prec" else (xfm_f, r_xfmf)
        P.op("vector", lambda v, src_t=src_t, pr=pr: v.tensor_sub(
            src_t[:], src_t[:], bcast(mean1[:, pr, :], 1, NCH)),
            reads=[r_mean1[pr]], writes=[r_srct])
        P.op("vector", lambda v, src_t=src_t, pr=pr: v.tensor_mul(
            src_t[:], src_t[:], bcast(inv1[:, pr, :], 1, NCH)),
            reads=[r_inv1[pr]], writes=[r_srct])
        P.op("vector", lambda v, src_t=src_t: v.tensor_mul(
            src_t[:], src_t[:], bcast(lnv[:, 0:16], 2, 128)),
            reads=[r_vecs], writes=[r_srct])
        P.op("vector", lambda v, src_t=src_t, d=dst_xt: v.tensor_add(
            d[:], src_t[:], bcast(lnv[:, 16:32], 2, 128)),
            reads=[r_srct, r_vecs], writes=[r_dst])

    for sweep in range(4):
        own = (sweep == 3)
        for j in range(8):
            rh = r_xt_sweep[j // 4]
            if own:
                frontend_tile("own", xo_d[j],
                              xt_sweep[:, :, j * 128:(j + 1) * 128], rh)
            else:
                tile = sweep * 8 + j
                frontend_tile("prec", xp_d[tile],
                              xt_sweep[:, :, j * 128:(j + 1) * 128], rh)

        if own:
            def evac(m, pst, prs, ti):
                g0 = (ti * 512) // L
                src = pst[:, 0:512].rearrange("p (g i) -> p i g", i=L)
                P.op("scalar", lambda sc, src=src, m=m, g0=g0: sc.activation(
                    u_own[:, m, :, g0:g0 + 16], src, AF.Identity,
                    bias=lncol("inb", m), scale=1.0),
                    reads=[prs, r_vecs], writes=[r_uo])
        else:
            def evac(m, pst, prs, ti, sweep=sweep):
                g0 = (sweep * 1024 + ti * 512) // L
                src = pst[:, 0:512].rearrange("p (g i) -> p i g", i=L)
                P.op("scalar", lambda sc, src=src, m=m, g0=g0: sc.activation(
                    u_prec[:, m, :, g0:g0 + 16], src, AF.Identity,
                    bias=lncol("inb", m), scale=1.0),
                    reads=[prs, r_vecs], writes=[r_up])

        proj_sweep(wr16, r_wr16, in_w_d, D, range(H), [(0, 512), (512, 512)],
                   lambda k, t0, tw: xt_sweep[:, k, t0:t0 + tw],
                   lambda t0: [r_xt_sweep[t0 // 512]], evac)

    if tap("u_own", (128, H, L, NGO), BF16):
        P.op("sync", lambda s: s.dma_start(out=tap_d["u_own"], in_=u_own[:]),
             reads=[r_uo], dma="tap1")
    if tap("u_prec", (128, H, L, NGP), BF16):
        P.op("sync", lambda s: s.dma_start(out=tap_d["u_prec"], in_=u_prec[:]),
             reads=[r_up], dma="tap2")
    if tap("xt_own", (128, NCH, TC), BF16):
        P.op("sync", lambda s: s.dma_start(out=tap_d["xt_own"], in_=xt_sweep[:]),
             reads=r_xt_sweep, dma="tap3")

    if stage == "uproj":
        P.op("gpsimd", lambda g: g.dma_start(
            out=out_d[0:128, 0:512],
            in_=u_own[:, 0, :, :].rearrange("p i g -> p (i g)")[:, 0:512]),
            reads=[r_uo], dma="out")
        fsc.close()
        usc.close()
        finish(nc, ctx, P, perm)
        return nc, tap_d

    P.gate()
    fsc.close()
    ysc = Scope(nc)
    y_sb = ysc.sb("y_sb", [128, H, TC], BF16, side="right")
    r_y = res("y_sb")
    ssc = Scope(nc)
    phSB = [ssc.psum(f"phS{i}", [128, TC], F32) for i in range(2)]
    pvy = [ssc.psum(f"pvy{i}", [128, 512], F32) for i in range(2)]
    paggB = [ssc.psum(f"pagg{i}", [128, 512], F32) for i in range(2)]
    r_phSB = [res("phS0"), res("phS1")]
    r_pvy = [res("pvy0"), res("pvy1")]
    r_paggB = [res("pagg0"), res("pagg1")]
    trctr = [0]

    abcd = [ssc.sb("abcd0", [128, 4 * 128], BF16)] * 2
    PtBB = [ssc.sb(f"Pt{i}", [128, (L + 1) * 128], BF16) for i in range(2)]
    PPtB1 = ssc.sb("PPt", [128, (L + 1) * 128], BF16)
    trs1 = ssc.sb("trs", [128, 5 * 128], BF16)
    v_pre = [ssc.sb(f"v_pre{i}", [128, L * NGP], BF16) for i in range(2)]
    v_ownb = ssc.sb("v_ownb", [128, TC], BF16)
    h_bf = ssc.sb("h_bf", [128, TC], BF16)
    ntop = [ssc.sb(f"ntop{i}", [128, 128], BF16) for i in range(2)]
    r_ntop = [res("ntop0"), res("ntop1")]
    S_pre = ssc.sb("S_pre", [128, NGP], BF16)
    S_ext = ssc.sb("S_ext", [128, L], BF16)
    Hprev = ssc.sb("Hprev", [128, NGO], BF16)
    Zc = ssc.sb("Zc", [128, 4], BF16)
    r_abcd = [res("abcd0")] * 2
    r_PtB = [res("Pt0"), res("Pt1")]
    r_PPt1 = res("PPt")
    r_trs1 = {nm: res("trs_" + nm) for nm in ("Bt", "Ct", "Dt", "PPQ", "A1024n")}
    r_vpre = [res("v_pre0"), res("v_pre1")]
    r_vown = res("v_ownb")
    r_hbf = res("h_bf")
    r_Spre = res("S_pre")
    r_Sext = res("S_ext")
    r_Hprev = res("Hprev")
    r_Zc = res("Zc")

    TRS = {"Bt": 0, "Ct": 1, "Dt": 2, "PPQ": 3, "A1024n": 4}

    def ptk(buf, k):
        return buf[:, k * 128:(k + 1) * 128]

    def transpose_to(dst_ap, dst_res, src_ap, src_res, slot=None):
        sl = trctr[0] % 2
        trctr[0] += 1
        P.op("tensor", lambda t, s=src_ap, sl=sl: t.matmul(
            pvy[sl][:, 0:128], s, ident_bf[:], start=True, stop=True,
            skip_group_check=True),
            reads=[src_res, r_const], writes=[r_pvy[sl]])
        P.op("scalar", lambda sc, d=dst_ap, sl=sl: sc.copy(
            d, pvy[sl][:, 0:128]), reads=[r_pvy[sl]], writes=[dst_res])

    def emit_head(h):
        hb = h % 2
        A_sb = abcd[hb][:, 0:128]
        B_sb = abcd[hb][:, 128:256]
        C_sb = abcd[hb][:, 256:384]
        D_sb = abcd[hb][:, 384:512]
        myv, r_myv = v_pre[hb], r_vpre[hb]
        myph, r_myph = phSB[hb], r_phSB[hb]
        pagg, r_pagg = paggB[hb], r_paggB[hb]
        myntop, r_myntop = ntop[hb], r_ntop[hb]
        PtB, r_Pt = PtBB[hb], r_PtB[hb]
        PPtB, r_PPt = PPtB1, r_PPt1
        trs, r_trs = trs1, r_trs1

        def trsl(nm):
            o = TRS[nm] * 128
            return trs[:, o:o + 128]

        def chain_doubling(buf, r_buf, base_nat, r_base):
            P.op("scalar", lambda sc, b=base_nat: sc.copy(myntop[:], b),
                 reads=[r_base], writes=[r_myntop])
            P.op("tensor", lambda t, b=base_nat: t.matmul(
                pvy[0][:, 0:128], b, ident_bf[:], start=True, stop=True,
                skip_group_check=True),
                reads=[r_base, r_const], writes=[r_pvy[0]])
            P.op("scalar", lambda sc, buf=buf: sc.copy(
                ptk(buf, 1), pvy[0][:, 0:128]), reads=[r_pvy[0]], writes=[r_buf])
            kk = 1
            while kk < L:
                nn = min(kk, L - kk)
                done = 0
                while done < nn:
                    take = min(4, nn - done)
                    pb = (done // 4) % 2
                    P.op("tensor", lambda t, done=done, take=take, pb=pb,
                         buf=buf:
                         t.matmul(pvy[pb][:, 0:take * 128], myntop[:],
                                  buf[:, (1 + done) * 128:(1 + done + take) * 128],
                                  start=True, stop=True, skip_group_check=True),
                         reads=[r_myntop, r_buf], writes=[r_pvy[pb]])
                    P.op("scalar", lambda sc, kk=kk, done=done, take=take, pb=pb,
                         buf=buf: sc.copy(
                             buf[:, (kk + 1 + done) * 128:(kk + 1 + done + take) * 128],
                             pvy[pb][:, 0:take * 128]),
                         reads=[r_pvy[pb]], writes=[r_buf])
                    done += take
                if kk + nn < L:
                    P.op("tensor", lambda t, buf=buf, nn=nn: t.matmul(
                        pvy[1][:, 0:128], ptk(buf, nn), myntop[:],
                        start=True, stop=True, skip_group_check=True),
                        reads=[r_buf, r_myntop], writes=[r_pvy[1]])
                    P.op("scalar", lambda sc: sc.copy(myntop[:], pvy[1][:, 0:128]),
                         reads=[r_pvy[1]], writes=[r_myntop])
                kk += nn

        for idx, src in ((0, Am_d), (1, Bm_d), (2, Cm_d), (3, Dm_d)):
            P.op("gpsimd", lambda g, idx=idx, src=src: g.dma_start(
                out=abcd[hb][:, idx * 128:(idx + 1) * 128], in_=src[h, :, :]),
                writes=[r_abcd[hb]], dma=r_abcd[hb].name)

        transpose_to(trsl("Bt"), r_trs["Bt"], B_sb, r_abcd[hb])
        transpose_to(trsl("Ct"), r_trs["Ct"], C_sb, r_abcd[hb])
        transpose_to(trsl("Dt"), r_trs["Dt"], D_sb, r_abcd[hb])

        P.op("vector", lambda v: v.tensor_copy(ptk(PtB, 0), ident_bf[:]),
             reads=[r_const], writes=[r_Pt])
        chain_doubling(PtB, r_Pt, A_sb, r_abcd[hb])

        uof = u_own[:, h, :, :].rearrange("p i g -> p (i g)")
        for half in range(2):
            pb = half % 2
            P.op("tensor", lambda t, half=half, pb=pb, uof=uof: t.matmul(
                pvy[pb][:], trsl("Bt"), uof[:, half * 512:(half + 1) * 512],
                start=True, stop=True, skip_group_check=True),
                reads=[r_trs["Bt"], r_uo], writes=[r_pvy[pb]])
            P.op("scalar", lambda sc, half=half, pb=pb: sc.copy(
                v_ownb[:, half * 512:(half + 1) * 512], pvy[pb][:]),
                reads=[r_pvy[pb]], writes=[r_vown])

        upf = u_prec[:, h, :, :].rearrange("p i g -> p (i g)")
        for j in range(6):
            pb = j % 2
            P.op("tensor", lambda t, j=j, pb=pb: t.matmul(
                pvy[pb][:], trsl("Bt"), upf[:, j * 512:(j + 1) * 512],
                start=True, stop=True, skip_group_check=True),
                reads=[r_trs["Bt"], r_up], writes=[r_pvy[pb]])
            P.op("scalar", lambda sc, j=j, pb=pb, myv=myv: sc.copy(
                myv[:, j * 512:(j + 1) * 512], pvy[pb][:]),
                reads=[r_pvy[pb]], writes=[r_myv])

        for i in range(L):
            P.op("tensor", lambda t, i=i: t.matmul(
                pagg[:, 0:NGP], ptk(PtB, L - 1 - i),
                myv[:, i * NGP:(i + 1) * NGP],
                start=(i == 0), stop=(i == L - 1), skip_group_check=True),
                reads=[r_Pt, r_myv], writes=([r_pagg] if i in (0, L - 1) else []))
        P.op("vector", lambda v: v.tensor_mul(S_pre[:], pagg[:, 0:NGP], maskb[:]),
             reads=[r_pagg, r_maskb], writes=[r_Spre])

        first_conv = True
        for k in range(L):
            c0 = k * NGO
            pieces = [(c0, 512), (512, TC)] if c0 < 512 else [(c0, TC)]
            for (a, bnd) in pieces:
                if a >= bnd:
                    continue
                last = (k == L - 1 and bnd == TC)
                P.op("tensor", lambda t, k=k, a=a, bnd=bnd, myph=myph: t.matmul(
                    myph[:, a:bnd], ptk(PtB, k),
                    v_ownb[:, a - k * NGO:bnd - k * NGO],
                    start=(k == 0), stop=False, skip_group_check=True),
                    reads=[r_Pt, r_vown],
                    writes=([r_myph] if (first_conv or last) else []))
                first_conv = False
        P.op("vector", lambda v, myph=myph: v.tensor_copy(
            S_ext[:, 1:L], myph[:, 992:1023]), reads=[r_myph], writes=[r_Sext])

        P.op("tensor", lambda t: t.matmul(pvy[0][:, 0:128], ptk(PtB, L // 2),
             myntop[:], start=True, stop=True, skip_group_check=True),
             reads=[r_Pt, r_myntop], writes=[r_pvy[0]])
        P.op("scalar", lambda sc: sc.copy(trsl("PPQ"), pvy[0][:, 0:128]),
             reads=[r_pvy[0]], writes=[r_trs["PPQ"]])
        P.op("vector", lambda v: v.tensor_copy(ptk(PPtB, 0), ident_bf[:]),
             reads=[r_const], writes=[r_PPt])
        chain_doubling(PPtB, r_PPt, trsl("PPQ"), r_trs["PPQ"])

        for m in range(L):
            P.op("tensor", lambda t, m=m: t.matmul(
                pagg[:, 256:259], ptk(PPtB, m), S_pre[:, 31 - m::L],
                start=(m == 0), stop=(m == L - 1), skip_group_check=True),
                reads=[r_PPt, r_Spre], writes=([r_pagg] if m in (0, L - 1) else []))
        P.op("vector", lambda v: v.tensor_copy(Zc[:, 0:3], pagg[:, 256:259]),
             reads=[r_pagg], writes=[r_Zc])

        P.op("tensor", lambda t: t.matmul(pvy[1][:, 0:128], ptk(PPtB, L // 2),
             myntop[:], start=True, stop=True, skip_group_check=True),
             reads=[r_PPt, r_myntop], writes=[r_pvy[1]])
        P.op("scalar", lambda sc: sc.copy(trsl("A1024n"), pvy[1][:, 0:128]),
             reads=[r_pvy[1]], writes=[r_trs["A1024n"]])
        P.op("tensor", lambda t: t.matmul(pagg[:, 300:428], trsl("A1024n"),
             ptk(PPtB, L), start=True, stop=True, skip_group_check=True),
             reads=[r_trs["A1024n"], r_PPt], writes=[r_pagg])
        P.op("vector", lambda v: v.tensor_copy(trsl("PPQ"), pagg[:, 300:428]),
             reads=[r_pagg], writes=[r_trs["PPQ"]])
        P.op("tensor", lambda t: t.matmul(pagg[:, 448:449], trsl("PPQ"),
             Zc[:, 0:1], start=True, stop=False, skip_group_check=True),
             reads=[r_trs["PPQ"], r_Zc], writes=[r_pagg])
        P.op("tensor", lambda t: t.matmul(pagg[:, 448:449], ptk(PPtB, L),
             Zc[:, 1:2], start=False, stop=False, skip_group_check=True),
             reads=[r_PPt, r_Zc])
        P.op("tensor", lambda t: t.matmul(pagg[:, 448:449], ident_bf[:],
             Zc[:, 2:3], start=False, stop=True, skip_group_check=True),
             reads=[r_const, r_Zc], writes=[r_pagg])
        P.op("vector", lambda v: v.tensor_copy(S_ext[:, 0:1], pagg[:, 448:449]),
             reads=[r_pagg], writes=[r_Sext])

        for m in range(NGO):
            P.op("tensor", lambda t, m=m: t.matmul(
                pagg[:, 256 + m:256 + NGO], ptk(PPtB, m), S_ext[:, 0:NGO - m],
                start=(m == 0), stop=(m == NGO - 1), skip_group_check=True),
                reads=[r_PPt, r_Sext], writes=([r_pagg] if m in (0, NGO - 1) else []))
        P.op("vector", lambda v: v.tensor_copy(Hprev[:], pagg[:, 256:256 + NGO]),
             reads=[r_pagg], writes=[r_Hprev])

        for i in range(L):
            first = (i == 0)
            last = (i == L - 1)
            P.op("tensor", lambda t, i=i, myph=myph: t.matmul(
                myph[:, i * NGO:(i + 1) * NGO], ptk(PtB, i + 1), Hprev[:],
                start=False, stop=last, skip_group_check=True),
                reads=[r_Pt, r_Hprev],
                writes=([r_myph] if (first or last) else []))

        for half in range(2):
            P.op("scalar", lambda sc, half=half, myph=myph: sc.copy(
                h_bf[:, half * 512:(half + 1) * 512],
                myph[:, half * 512:(half + 1) * 512]),
                reads=[r_myph], writes=[r_hbf])
        yv = y_sb[:, h, :].rearrange("p (g i) -> p i g", i=L)
        for half in range(2):
            pb = half % 2
            P.op("tensor", lambda t, half=half, pb=pb: t.matmul(
                pvy[pb][:], trsl("Ct"), h_bf[:, half * 512:(half + 1) * 512],
                start=True, stop=False, skip_group_check=True),
                reads=[r_trs["Ct"], r_hbf], writes=[r_pvy[pb]])
            P.op("tensor", lambda t, half=half, pb=pb, uof=uof: t.matmul(
                pvy[pb][:], trsl("Dt"), uof[:, half * 512:(half + 1) * 512],
                start=False, stop=True, skip_group_check=True),
                reads=[r_trs["Dt"], r_uo], writes=[r_pvy[pb]])
            P.op("scalar", lambda sc, half=half, pb=pb, yv=yv: sc.copy(
                yv[:, half * 16:(half + 1) * 16, :],
                pvy[pb][:].rearrange("p (i g) -> p i g", g=NGO)),
                reads=[r_pvy[pb]], writes=[r_y])

    for h in range(H):
        emit_head(h)

    if tap("y", (128, H, TC), BF16):
        P.op("sync", lambda s: s.dma_start(out=tap_d["y"], in_=y_sb[:]),
             reads=[r_y], dma="tap4")

    if stage == "ssm":
        P.op("gpsimd", lambda g: g.dma_start(
            out=out_d[0:128, 0:512], in_=y_sb[:, 0, 0:512]),
            reads=[r_y], dma="out")
        ssc.close()
        ysc.close()
        usc.close()
        finish(nc, ctx, P, perm)
        return nc, tap_d

    P.gate()
    ssc.close()
    usc.close()

    tsc = Scope(nc)
    mm2 = [tsc.psum(f"mm2_{i}", [128, 512], F32) for i in range(2)]
    sums = tsc.psum("sums", [128, 2, TC], F32)
    trfT = tsc.psum("trfT", [128, 128], F32)
    trbT = tsc.psum("trbT", [128, 128], BF16)
    r_mm2 = [res("mm2_0"), res("mm2_1")]
    r_sums2 = res("sums2")
    r_trfT = res("trfT")
    r_trbT = res("trbT")

    meanF = tsc.sb("meanF", [128, TC], F32)
    invF = tsc.sb("invF", [128, TC], F32)
    x2c = tsc.sb("x2c", [128, TC], F32)
    x2bfc = tsc.sb("x2bfc", [128, TC], BF16)
    wrT = [tsc.sb(f"wrT{i}", [128, 16, 128], BF16) for i in range(3)]
    xr = [tsc.sb(f"xr{i}", [128, TC], F32) for i in range(2)]
    r_meanF, r_invF = res("meanF"), res("invF")
    r_x2c, r_x2bfc = res("x2c"), res("x2bfc")
    r_wrT = [res(f"wrT{i}") for i in range(3)]
    r_xr = [res("xr0"), res("xr1")]
    r_sx2d = res("sx2_d")
    r_sspd = res("ssp_d")

    def stats_full(m, last_m, src_bf_ap, r_src):
        for half in range(2):
            t0 = half * 512
            stats_chunk(sums[:, 0, t0:t0 + 512], sums[:, 1, t0:t0 + 512],
                        r_sums2, src_bf_ap[:, t0:t0 + 512], r_src, 512,
                        first=(m == 0), last=(m == last_m), sq0=0)

    def stats_full_finish():
        for half in range(2):
            t0 = half * 512
            stats_finish(sums[:, 0, t0:t0 + 512], sums[:, 1, t0:t0 + 512],
                         r_sums2, meanF[:, t0:t0 + 512], invF[:, t0:t0 + 512],
                         x2c[:, t0:t0 + 512], r_meanF, r_invF, r_x2c)

    def evac_op(m, pst, prs, ti):
        xrb, r_xrb = xr[m % 2], r_xr[m % 2]
        if ti == 0:
            P.op("sync", lambda s, m=m, d=xrb: s.dma_start(
                out=d[:].rearrange("p (a t) -> p a t", a=8), in_=xo_d[:, :, m, :].rearrange("a p t -> p a t")), reads=[],
                writes=[r_xrb], dma=r_xrb.name)
        t0 = ti * 512
        P.op("vector", lambda v, m=m, t0=t0, pst=pst, xrb=xrb: v.scalar_tensor_tensor(
            x2c[:, t0:t0 + 512], pst[:, 0:512], lncol("outb", m),
            xrb[:, t0:t0 + 512], ALU.add, ALU.add),
            reads=[prs, r_vecs, r_xrb], writes=[r_x2c])
        if ti == 1:
            P.op("scalar", lambda sc: sc.copy(x2bfc[:], x2c[:]),
                 reads=[r_x2c], writes=[r_x2bfc])
            stats_full(m, NCH - 1, x2bfc, r_x2bfc)
            P.op("sync", lambda s, m=m: s.dma_start(out=sx2_d[:, m, :],
                 in_=x2c[:]), reads=[r_x2c], writes=[r_sx2d], dma="sx2")

    mmP, r_mmP = mm2, r_mm2
    proj_sweep(wrT, r_wrT, out_w_d, D, range(NCH), [(0, 512), (512, 512)],
               lambda k, t0, tw: y_sb[:, k, t0:t0 + tw], [r_y], evac_op)
    stats_full_finish()

    if tap("x2", (128, NCH, TC)):
        P.op("sync", lambda s: s.dma_start(out=tap_d["x2"], in_=sx2_d[:]),
             reads=[r_sx2d], dma="tap5")

    P.gate()
    ysc.close()
    t1sc = Scope(nc)
    xt2 = t1sc.sb("xt2", [128, NCH, TC], BF16, side="right")
    r_xt2 = res("xt2")
    lnslots = [(xr[0], r_xr[0], "sync"), (xr[1], r_xr[1], "scalar"),
               (x2c, r_x2c, "sync")]
    for m in range(NCH):
        xrb, r_xrb, qeng = lnslots[m % 3]
        P.op(qeng, lambda s, m=m, d=xrb: s.dma_start(
            out=d[:], in_=sx2_d[:, m, :]), reads=[r_sx2d],
            writes=[r_xrb], dma=r_xrb.name)
        P.op("vector", lambda v, xrb=xrb: v.tensor_sub(
            xrb[:], xrb[:], meanF[:]), reads=[r_meanF], writes=[r_xrb])
        P.op("vector", lambda v, xrb=xrb: v.tensor_mul(
            xrb[:], xrb[:], invF[:]), reads=[r_invF], writes=[r_xrb])
        P.op("scalar", lambda sc, m=m, xrb=xrb: sc.activation(
            xt2[:, m, :], xrb[:], AF.Identity,
            bias=lncol("ln2b", m), scale=lncol("ln2w", m)),
            reads=[r_xrb, r_vecs], writes=[r_xt2])
    if tap("xt2", (128, NCH, TC), BF16):
        P.op("sync", lambda s: s.dma_start(out=tap_d["xt2"], in_=xt2[:]),
             reads=[r_xt2], dma="tap6")

    h_ffn = t1sc.sb("h_ffn", [128, NH, TC], BF16)
    r_hffn = res("h_ffn")

    def evac_f1(m, pst, prs, ti):
        t0 = ti * 512
        P.op("scalar", lambda s, m=m, t0=t0, pst=pst: s.activation(
            h_ffn[:, m, t0:t0 + 512], pst[:, 0:512], AF.Gelu,
            bias=b1s[:, m:m + 1], scale=1.0),
            reads=[prs, r_vecs], writes=[r_hffn])

    proj_sweep(wrT, r_wrT, w1_d, D, range(NH), [(0, 512), (512, 512)],
               lambda k, t0, tw: xt2[:, k, t0:t0 + tw], [r_xt2], evac_f1)

    P.gate()
    t1sc.close_right()
    t2sc = Scope(nc)
    wr64 = [t2sc.sb(f"wr64_{i}", [128, 64, 128], BF16, side="right")
            for i in range(2)]
    r_wr64 = [res("wr64_0"), res("wr64_1")]
    xr2 = [t2sc.sb(f"xr2{i}", [128, TC], F32) for i in range(2)]
    r_xr2 = [res("xr2_0"), res("xr2_1")]

    def evac_f2(m, pst, prs, ti):
        xrb, r_xrb = xr[m % 2], r_xr[m % 2]
        xob, r_xob = xr2[m % 2], r_xr2[m % 2]
        if ti == 0:
            P.op("sync", lambda s, m=m, d=xrb: s.dma_start(
                out=d[:], in_=sx2_d[:, m, :]), reads=[r_sx2d],
                writes=[r_xrb], dma=r_xrb.name)
            P.op("sync", lambda s, m=m, d=xob: s.dma_start(
                out=d[:].rearrange("p (a t) -> p a t", a=8), in_=xo_d[:, :, m, :].rearrange("a p t -> p a t")), reads=[],
                writes=[r_xob], dma=r_xob.name)
        t0 = ti * 512
        P.op("vector", lambda v, m=m, t0=t0, pst=pst, xrb=xrb: v.scalar_tensor_tensor(
            x2c[:, t0:t0 + 512], pst[:, 0:512], b2s[:, m:m + 1],
            xrb[:, t0:t0 + 512], ALU.add, ALU.add),
            reads=[prs, r_vecs, r_xrb], writes=[r_x2c])
        P.op("vector", lambda v, t0=t0, xob=xob: v.tensor_add(
            x2c[:, t0:t0 + 512], x2c[:, t0:t0 + 512], xob[:, t0:t0 + 512]),
            reads=[r_xob], writes=[r_x2c])
        if ti == 1:
            P.op("scalar", lambda sc: sc.copy(x2bfc[:], x2c[:]),
                 reads=[r_x2c], writes=[r_x2bfc])
            stats_full(m, NCH - 1, x2bfc, r_x2bfc)
            P.op("sync", lambda s, m=m: s.dma_start(out=ssp_d[:, m, :],
                 in_=x2c[:]), reads=[r_x2c], writes=[r_sspd], dma="ssp")

    proj_sweep(wr64, r_wr64, w2_d, HID, range(NCH), [(0, 512), (512, 512)],
               lambda k, t0, tw: h_ffn[:, k, t0:t0 + tw], [r_hffn], evac_f2)
    stats_full_finish()

    P.gate()
    t2sc.close()
    t1sc.close()
    t3sc = Scope(nc)
    xA = t3sc.sb("xA", [128, NCH, TC], F32)
    he = t3sc.sb("he", [128, NSH, TC], BF16)
    Cb = t3sc.sb("Cb", [128, TC], BF16)
    rsc = t3sc.sb("rsc", [128, 8], F32)
    rs2 = t3sc.sb("rs2", [128, 12], F32)
    combbf = t3sc.sb("combbf", [128, 8], BF16)
    combT = t3sc.sb("combT", [8, TC], BF16)
    selE = t3sc.sb("selE", [8, 8 * 128], BF16)
    wrb = t3sc.sb("wrb", [128, NCH, 8], BF16)
    tsub = Scope(nc)
    t1b = tsub.sb("t1b", [128, TC], BF16)
    t2b = tsub.sb("t2b", [128, TC], BF16)
    xAbf = t3sc.sb("xAbf", [128, NCH, TC], BF16, side="right")
    wrM = [t3sc.sb(f"wrM{i}", [128, 22, 128], BF16, side="right")
           for i in range(3)]
    r_xAp = [res("xA_ev"), res("xA_od")]
    r_he = res("he")
    r_Cb = res("Cb")
    r_t1b, r_t2b = res("t1b"), res("t2b")
    r_rsc, r_rs2 = res("rsc"), res("rs2")
    r_combbf, r_combT = res("combbf"), res("combT")
    r_selE = res("selE")
    r_wrb = res("wrb")
    r_xAbf = res("xAbf")
    r_wrM = [res(f"wrM{i}") for i in range(3)]

    P.op("gpsimd", lambda g: g.memset(selE[:], 0.0), writes=[r_selE])
    P.op("gpsimd", lambda g: g.affine_select(
        out=selE[:].rearrange("p (e i) -> p e i", i=128),
        in_=selE[:].rearrange("p (e i) -> p e i", i=128),
        compare_op=ALU.not_equal, fill=1.0, base=0,
        pattern=[[-1, E], [0, 128]], channel_multiplier=1), writes=[r_selE])
    P.op("gpsimd", lambda g: g.dma_start(
        out=wrb[:], in_=wr_d.rearrange("(k p) e -> p k e", p=128)),
        writes=[r_wrb], dma="wrb")

    for m in range(NCH):
        xrb, r_xrb, qeng = lnslots[m % 3]
        P.op(qeng, lambda s, m=m, d=xrb: s.dma_start(
            out=d[:], in_=ssp_d[:, m, :]), reads=[r_sspd],
            writes=[r_xrb], dma=r_xrb.name)
        P.op("vector", lambda v, xrb=xrb: v.tensor_sub(
            xrb[:], xrb[:], meanF[:]), reads=[r_meanF], writes=[r_xrb])
        P.op("vector", lambda v, m=m, xrb=xrb: v.tensor_mul(
            xA[:, m, :], xrb[:], invF[:]), reads=[r_xrb, r_invF],
            writes=[r_xAp[m % 2]])
        P.op("scalar", lambda sc, m=m: sc.copy(xAbf[:, m, :], xA[:, m, :]),
             reads=[r_xAp[m % 2]], writes=[r_xAbf])
    if tap("xA", (128, NCH, TC)):
        P.op("sync", lambda s: s.dma_start(out=tap_d["xA"], in_=xA[:]),
             reads=[r_xAp[0], r_xAp[1]], dma="tap7")

    for tt in range(8):
        tt0 = tt * 128
        for k in range(NCH):
            P.op("tensor", lambda t, k=k, tt0=tt0: t.matmul(
                trfT[:, 0:8], xAbf[:, k, tt0:tt0 + 128], wrb[:, k, :],
                start=(k == 0), stop=(k == NCH - 1), skip_group_check=True),
                reads=[r_xAbf, r_wrb],
                writes=([r_trfT] if k in (0, NCH - 1) else []))
        P.op("vector", lambda v: v.tensor_copy(rsc[:], trfT[:, 0:8]),
             reads=[r_trfT], writes=[r_rsc])
        P.op("vector", lambda v: v.reduce_max(rs2[:, 0:1], rsc[:], axis=AX.X,
             negate=True), reads=[r_rsc], writes=[r_rs2])
        P.op("scalar", lambda s: s.activation(rsc[:], rsc[:], AF.Exp,
             bias=rs2[:, 0:1], scale=1.0), reads=[r_rs2], writes=[r_rsc])
        P.op("vector", lambda v: v.reduce_sum(rs2[:, 1:2], rsc[:], axis=AX.X),
             reads=[r_rsc], writes=[r_rs2])
        P.op("vector", lambda v: v.reciprocal(rs2[:, 2:3], rs2[:, 1:2]),
             writes=[r_rs2])
        P.op("vector", lambda v: v.tensor_scalar_mul(rsc[:], rsc[:],
             rs2[:, 2:3]), reads=[r_rs2], writes=[r_rsc])
        P.op("vector", lambda v: v.reduce_max(rs2[:, 3:4], rsc[:], axis=AX.X),
             reads=[r_rsc], writes=[r_rs2])
        P.op("vector", lambda v: v.tensor_single_scalar(rs2[:, 4:12], rsc[:],
             rs2[:, 3:4], ALU.is_ge), reads=[r_rsc], writes=[r_rs2])
        P.op("vector", lambda v: v.scalar_tensor_tensor(rs2[:, 4:12],
             rs2[:, 4:12], -1e30, rsc[:], ALU.mult, ALU.add),
             reads=[r_rsc], writes=[r_rs2])
        P.op("vector", lambda v: v.reduce_max(rs2[:, 3:4], rs2[:, 4:12],
             axis=AX.X), writes=[r_rs2])
        P.op("vector", lambda v: v.tensor_single_scalar(rs2[:, 4:12], rsc[:],
             rs2[:, 3:4], ALU.is_ge), reads=[r_rsc], writes=[r_rs2])
        P.op("vector", lambda v: v.tensor_mul(combbf[:], rsc[:], rs2[:, 4:12]),
             reads=[r_rsc, r_rs2], writes=[r_combbf])
        P.op("tensor", lambda t: t.transpose(
            trbT[0:8, :], combbf[:], ident_bf[:]),
            reads=[r_combbf, r_const], writes=[r_trbT])
        P.op("vector", lambda v, tt0=tt0: v.tensor_copy(
            combT[:, tt0:tt0 + 128], trbT[0:8, :]),
            reads=[r_trbT], writes=[r_combT])

    def gateup_sweep(wg_hbm, wu_hbm, n_m, kdim, cb_scale, dst, r_dst):
        nk = kdim // 128
        b0 = moe_base[0]
        for m in range(n_m):
            wg, r_wg = ws_moe.get(b0 + 2 * m)
            wu, r_wu = ws_moe.get(b0 + 2 * m + 1)
            for ti in range(2):
                t0 = ti * 512
                pg, r_pg = mm2[0], r_mm2[0]
                pu, r_pu = mm2[1], r_mm2[1]
                for k in range(nk):
                    P.op("tensor", lambda t, k=k, t0=t0, wg=wg: t.matmul(
                        pg[:, 0:512], wg[:, k, :], xAbf[:, k, t0:t0 + 512],
                        start=(k == 0), stop=(k == nk - 1),
                        skip_group_check=True),
                        reads=[r_wg, r_xAbf],
                        writes=([r_pg] if k in (0, nk - 1) else []))
                P.op("scalar", lambda s, t0=t0: s.activation(
                    t1b[:, t0:t0 + 512], pg[:, 0:512], AF.Silu),
                    reads=[r_pg], writes=[r_t1b])
                for k in range(nk):
                    P.op("tensor", lambda t, k=k, t0=t0, wu=wu: t.matmul(
                        pu[:, 0:512], wu[:, k, :], xAbf[:, k, t0:t0 + 512],
                        start=(k == 0), stop=(k == nk - 1),
                        skip_group_check=True),
                        reads=[r_wu, r_xAbf],
                        writes=([r_pu] if k in (0, nk - 1) else []))
                if cb_scale:
                    P.op("vector", lambda v, t0=t0: v.tensor_mul(
                        t2b[:, t0:t0 + 512], pu[:, 0:512], Cb[:, t0:t0 + 512]),
                        reads=[r_pu, r_Cb], writes=[r_t2b])
                else:
                    P.op("vector", lambda v, t0=t0: v.tensor_copy(
                        t2b[:, t0:t0 + 512], pu[:, 0:512]),
                        reads=[r_pu], writes=[r_t2b])
                P.op("vector", lambda v, m=m, t0=t0, dst=dst: v.tensor_mul(
                    dst[:, m, t0:t0 + 512], t1b[:, t0:t0 + 512],
                    t2b[:, t0:t0 + 512]),
                    reads=[r_t1b, r_t2b], writes=[r_dst])
        moe_base[0] += 2 * n_m

    def down_sweep(wd_hbm, nk, src, r_src):
        b0 = moe_base[0]
        for m in range(NCH):
            wd, r_wd = ws_moe.get(b0 + m)
            for ti in range(2):
                t0 = ti * 512
                pb = ti % 2
                pst, prs = mm2[pb], r_mm2[pb]
                for k in range(nk):
                    P.op("tensor", lambda t, k=k, t0=t0, wd=wd, src=src,
                         pst=pst: t.matmul(
                        pst[:, 0:512], wd[:, k, :], src[:, k, t0:t0 + 512],
                        start=(k == 0), stop=(k == nk - 1),
                        skip_group_check=True),
                        reads=[r_wd, r_src],
                        writes=([prs] if k in (0, nk - 1) else []))
                P.op("vector", lambda v, m=m, t0=t0, pst=pst: v.tensor_add(
                    xA[:, m, t0:t0 + 512], xA[:, m, t0:t0 + 512],
                    pst[:, 0:512]), reads=[prs], writes=[r_xAp[m % 2]])
        moe_base[0] += NCH

    moe_wl = []
    for m in range(NSH):
        moe_wl.append(wsg_d[m])
        moe_wl.append(wsu_d[m])
    moe_wl += [wsd_d[m] for m in range(NCH)]
    for e in range(E):
        for m in range(NEH):
            moe_wl.append(weg_d[e][m])
            moe_wl.append(weu_d[e][m])
        moe_wl += [wed_d[e][m] for m in range(NCH)]
    ws_moe = WStream(wrM, r_wrM, moe_wl, ahead=len(wrM) - 1)
    moe_base = [0]

    gateup_sweep(wsg_d, wsu_d, NSH, D, False, he, r_he)
    down_sweep(wsd_d, NSH, he, r_he)

    for e in range(E):
        for half in range(2):
            t0 = half * 512
            pb = half % 2
            P.op("tensor", lambda t, e=e, t0=t0, pb=pb: t.matmul(
                mm2[pb][:, 0:512], selE[:, e * 128:(e + 1) * 128],
                combT[:, t0:t0 + 512], start=True, stop=True,
                skip_group_check=True),
                reads=[r_selE, r_combT], writes=[r_mm2[pb]])
            P.op("vector", lambda v, t0=t0, pb=pb: v.tensor_copy(
                Cb[:, t0:t0 + 512], mm2[pb][:, 0:512]),
                reads=[r_mm2[pb]], writes=[r_Cb])
        gateup_sweep(weg_d[e], weu_d[e], NEH, D, True, he, r_he)
        down_sweep(wed_d[e], NEH, he, r_he)

    tsub.close()
    for m in range(NCH):
        P.op("scalar", lambda sc, m=m: sc.copy(x2bfc[:], xA[:, m, :]),
             reads=[r_xAp[m % 2]], writes=[r_x2bfc])
        stats_full(m, NCH - 1, x2bfc, r_x2bfc)
    stats_full_finish()

    otm = [t3sc.sb(f"otm{i}", [128, 512], F32) for i in range(2)]
    r_otm = [res("otm0"), res("otm1")]
    for qd in range(4):
        for cc in range(4):
            m = qd * 4 + cc
            P.op("vector", lambda v, m=m: v.tensor_sub(
                xA[:, m, :], xA[:, m, :], meanF[:]), reads=[r_meanF],
                writes=[r_xAp[m % 2]])
            P.op("vector", lambda v, m=m: v.tensor_mul(
                xA[:, m, :], xA[:, m, :], invF[:]), reads=[r_invF],
                writes=[r_xAp[m % 2]])
        for blk in range(8):
            b0 = blk * 128
            ob = (qd * 8 + blk) % 2
            for cc in range(4):
                c = qd * 4 + cc
                sl = c % 2
                P.op("tensor", lambda t, c=c, b0=b0, sl=sl: t.transpose(
                    mm2[sl][:, 0:128], xA[:, c, b0:b0 + 128], ident_f[:]),
                    reads=[r_xAp[c % 2], r_const], writes=[r_mm2[sl]])
                if cc % 2 == 0:
                    P.op("scalar", lambda sc, cc=cc, sl=sl, ob=ob: sc.copy(
                        otm[ob][:, cc * 128:(cc + 1) * 128], mm2[sl][:, 0:128]),
                        reads=[r_mm2[sl]], writes=[r_otm[ob]])
                else:
                    P.op("vector", lambda v, cc=cc, sl=sl, ob=ob: v.tensor_copy(
                        otm[ob][:, cc * 128:(cc + 1) * 128], mm2[sl][:, 0:128]),
                        reads=[r_mm2[sl]], writes=[r_otm[ob]])
            P.op("sync", lambda s, b0=b0, qd=qd, ob=ob: s.dma_start(
                out=out_d[b0:b0 + 128, qd * 512:(qd + 1) * 512],
                in_=otm[ob][:]), reads=[r_otm[ob]], dma="out")

    t3sc.close()
    tsc.close()
    finish(nc, ctx, P, perm)
    return nc, tap_d


def finish(nc, ctx, P, perm):
    P.ins["sync"].append([lambda s: s.nop(),
                          [("D", k, c) for k, c in P.dmacnt.items()],
                          False, None])
    P.finalize(nc, ctx)
    with nc.Block() as block:
        @block.tensor
        def _(t):
            P.run_engine("tensor", t)

        @block.vector
        def _(v):
            P.run_engine("vector", v)

        @block.scalar
        def _(s):
            P.run_engine("scalar", s)

        @block.sync
        def _(s):
            P.run_engine("sync", s)

        @block.gpsimd
        def _(g):
            P.run_engine("gpsimd", g)
    perm.close()
    ctx.close()


def make_in_maps(inputs):
    x = np.ascontiguousarray(inputs["L0_attn_x"])

    def vecN(v, n):
        return np.ascontiguousarray(np.asarray(v, np.float32).reshape(n, 128).T)

    def wcolmajor(w):
        w = np.asarray(w, np.float32)
        K, M = w.shape
        return np.ascontiguousarray(
            w.reshape(K // 128, 128, M // 128, 128).transpose(2, 1, 0, 3))

    def wcolmajor_e(w):
        return np.ascontiguousarray(
            np.stack([wcolmajor(w[e]) for e in range(w.shape[0])]))

    shared = {
        "ln1w": vecN(inputs["L0_attn_ln1_w"], 16), "ln1b": vecN(inputs["L0_attn_ln1_b"], 16),
        "ln2w": vecN(inputs["L0_attn_ln2_w"], 16), "ln2b": vecN(inputs["L0_attn_ln2_b"], 16),
        "inb": vecN(inputs["L0_attn_in_b"], 16), "outb": vecN(inputs["L0_attn_out_b"], 16),
        "b2v": vecN(inputs["L0_attn_b2"], 16), "b1v": vecN(inputs["L0_attn_b1"], 64),
        "in_w": wcolmajor(inputs["L0_attn_in_w"]),
        "out_w": wcolmajor(inputs["L0_attn_out_w"]),
        "w1": wcolmajor(inputs["L0_attn_w1"]),
        "w2": wcolmajor(inputs["L0_attn_w2"]),
        "wsg": wcolmajor(inputs["L0_ffn_Ws_gate"]),
        "wsu": wcolmajor(inputs["L0_ffn_Ws_up"]),
        "wsd": wcolmajor(inputs["L0_ffn_Ws_down"]),
        "weg": wcolmajor_e(inputs["L0_ffn_We_gate"]),
        "weu": wcolmajor_e(inputs["L0_ffn_We_up"]),
        "wed": wcolmajor_e(inputs["L0_ffn_We_down"]),
        "wr": np.ascontiguousarray(inputs["L0_ffn_W_router"]),
        "Am": np.ascontiguousarray(inputs["L0_attn_A"]),
        "Bm": np.ascontiguousarray(inputs["L0_attn_Bm"]),
        "Cm": np.ascontiguousarray(inputs["L0_attn_Cm"]),
        "Dm": np.ascontiguousarray(inputs["L0_attn_Dm"]),
    }
    def fmaj(a):
        nt = a.shape[0] // 128
        return np.ascontiguousarray(
            a.reshape(nt, 128, NCH, 128).transpose(0, 3, 2, 1))

    in_maps = []
    for c in range(8):
        b, q = c // 4, c % 4
        t0 = q * TC
        xo = fmaj(x[b, t0:t0 + TC])
        xpad = np.zeros((TP, D), dtype=np.float32)
        lo = t0 - TP
        v0 = max(0, -lo)
        if t0 > 0:
            xpad[v0:] = x[b, max(0, lo):t0]
        mask = np.zeros((1, NGP), dtype=np.float32)
        mask[0, v0 // L:] = 1.0
        m = dict(shared)
        m.update({"xo": xo, "xp": fmaj(xpad), "maskp": mask})
        in_maps.append(m)
    return in_maps


def kernel(**inputs):
    nc, _ = build_nc(stage="full")
    in_maps = make_in_maps(inputs)
    res = run_bass_kernel_spmd(nc, in_maps, core_ids=list(range(8)))
    out = np.zeros((B, T, D), dtype=np.float32)
    for c in range(8):
        b, q = c // 4, c % 4
        out[b, q * TC:(q + 1) * TC] = res.results[c]["out"]
    return out


# version 18
# speedup vs baseline: 1.0926x; 1.0154x over previous
import dataclasses
import numpy as np
from contextlib import ExitStack

import concourse.bass as bass
import concourse.mybir as mybir
from concourse.bass_utils import run_bass_kernel_spmd

F32 = mybir.dt.float32
BF16 = mybir.dt.bfloat16
AF = mybir.ActivationFunctionType
ALU = mybir.AluOpType
AX = mybir.AxisListType

D = 2048
NCH = 16
H = 16
B = 2
T = 4096
TC = 1024
TP = 3072
L = 32
NGO = TC // L
NGP = TP // L
HID = 8192
NH = 64
DS = 2816
NSH = 22
DE = 1408
NEH = 11
E = 8
EPS = 1e-5

def bcast(ap, dim, count):
    l = [list(x) for x in ap.ap]
    l.insert(dim, [0, count])
    return dataclasses.replace(ap, ap=l)


COMPUTE = ("tensor", "vector", "scalar")
DMAQ = ("sync", "gpsimd")
ENGINES = COMPUTE + DMAQ


class Res:
    __slots__ = ("w", "rs", "name")

    def __init__(self, name=""):
        self.w = None
        self.rs = []
        self.name = name


class Plan:

    def __init__(self):
        self.ins = {e: [] for e in ENGINES}
        self.floor = []
        self.dmacnt = {}
        self.dmakey = {}
        self.dram_keys = set()
        self.drain_gates = False

    def op(self, eng, builder, reads=(), writes=(), extra_waits=(), dma=None):
        deps = set(self.floor)
        for r in reads:
            if r.w is not None:
                deps.add(r.w)
        for wres in writes:
            if wres.w is not None:
                deps.add(wres.w)
            for t in wres.rs:
                deps.add(t)
        for t in extra_waits:
            if t is not None:
                deps.add(t)
        idx = len(self.ins[eng])
        tok = (eng, idx)
        waits = []
        seen_keys = {}
        for (e2, i2) in deps:
            key = self.dmakey.get((e2, i2))
            if key is not None:
                seen_keys[key] = self.dmacnt[key]
            else:
                waits.append(("C", e2, i2))
        for key, cnt in seen_keys.items():
            waits.append(("D", key, cnt))
        if dma is not None:
            self.dmacnt[dma] = self.dmacnt.get(dma, 0) + 1
            self.dmakey[tok] = dma
        self.ins[eng].append([builder, waits, dma is not None, dma])
        for r in reads:
            r.rs.append(tok)
        for wres in writes:
            wres.w = tok
            wres.rs = []
        return tok

    def gate(self):
        def last(e):
            return (e, len(self.ins[e]) - 1) if self.ins[e] else None
        tv = self.op("vector", lambda v: v.engine_nop(),
                     extra_waits=[last("vector")])
        ts_ = self.op("scalar", lambda s: s.nop(), extra_waits=[last("scalar")])
        tg = self.op("gpsimd", lambda g: g.nop(), extra_waits=[last("gpsimd")])
        tt = self.op("tensor", lambda t: t.nop(),
                     extra_waits=[tv, ts_, tg, last("tensor"), last("sync")])
        self.ins["tensor"][tt[1]][1].extend(
            ("D", k, c) for k, c in self.dmacnt.items()
            if self.drain_gates or k not in self.dram_keys)
        self.floor = [tt]
        return tt

    def finalize(self, nc, ctx):
        csem = {e: ctx.enter_context(nc.semaphore("c_" + e)) for e in ENGINES}
        dsem = {k: ctx.enter_context(nc.semaphore("k_" + k)) for k in self.dmacnt}
        for e in ENGINES:
            for ent in self.ins[e]:
                for w in ent[1]:
                    if w[0] == "C":
                        self.ins[w[1]][w[2]][2] = True
        val = {}
        for e in ENGINES:
            cc = 0
            for i, ent in enumerate(self.ins[e]):
                if ent[2] and ent[3] is None:
                    cc += 1
                val[(e, i)] = cc
        out = {e: [] for e in ENGINES}
        for e in ENGINES:
            hw = {}
            for builder, waits, flagged, key in self.ins[e]:
                byk = {}
                for w in waits:
                    if w[0] == "C":
                        k2 = ("C", w[1])
                        v = val[(w[1], w[2])]
                    else:
                        k2 = ("D", w[1])
                        v = w[2] * 16
                    if v > byk.get(k2, 0):
                        byk[k2] = v
                resolved = []
                for k2, v in byk.items():
                    if v > hw.get(k2, 0):
                        hw[k2] = v
                        sem = csem[k2[1]] if k2[0] == "C" else dsem[k2[1]]
                        resolved.append((sem, v))
                out[e].append((builder, resolved, flagged, key))
        self.final = out
        self.csem = csem
        self.dsem = dsem

    def run_engine(self, eng, engine):
        for builder, waits, flagged, key in self.final[eng]:
            for s, v in waits:
                engine.wait_ge(s, v)
            ins = builder(engine)
            if key is not None:
                ins.then_inc(self.dsem[key], 16)
            elif flagged:
                ins.then_inc(self.csem[eng], 1)


class Scope:

    def __init__(self, nc):
        self.nc = nc
        self.cms = []

    def sb(self, name, shape, dt, side=None):
        cm = self.nc.sbuf_tensor(name, list(shape), dt, side=side)
        h = cm.__enter__()
        self.cms.append((cm, side))
        return h

    def psum(self, name, shape, dt=F32):
        cm = self.nc.psum_tensor(name, list(shape), dt)
        h = cm.__enter__()
        self.cms.append((cm, "psum"))
        return h

    def close_right(self):
        keep = []
        for cm, side in reversed(self.cms):
            if side == "right":
                cm.__exit__(None, None, None)
            else:
                keep.append((cm, side))
        self.cms = list(reversed(keep))

    def close(self):
        for cm, side in reversed(self.cms):
            cm.__exit__(None, None, None)
        self.cms = []


def build_nc(stage="full", taps=()):
    taps = set(taps)
    nc = bass.Bass("TRN2", target_bir_lowering=False, debug=False)

    d_in = {}

    def din(name, shape):
        d_in[name] = nc.dram_tensor(name, list(shape), F32, kind="ExternalInput").ap()
        return d_in[name]

    xo_d = din("xo", (TC // 128, 128, NCH, 128))
    xp_d = din("xp", (TP // 128, 128, NCH, 128))
    din("maskp", (1, NGP))
    for nm in ("ln1w", "ln1b", "ln2w", "ln2b", "inb", "outb", "b2v"):
        din(nm, (128, 16))
    din("b1v", (128, 64))
    in_w_d = din("in_w", (NCH, 128, NCH, 128)); out_w_d = din("out_w", (NCH, 128, NCH, 128))
    w1_d = din("w1", (NH, 128, NCH, 128)); w2_d = din("w2", (NCH, 128, NH, 128))
    wsg_d = din("wsg", (NSH, 128, NCH, 128)); wsu_d = din("wsu", (NSH, 128, NCH, 128))
    wsd_d = din("wsd", (NCH, 128, NSH, 128))
    weg_d = din("weg", (E, NEH, 128, NCH, 128)); weu_d = din("weu", (E, NEH, 128, NCH, 128))
    wed_d = din("wed", (E, NCH, 128, NEH, 128))
    wr_d = din("wr", (D, E))
    Am_d = din("Am", (H, 128, 128)); Bm_d = din("Bm", (H, 128, 128))
    Cm_d = din("Cm", (H, 128, 128)); Dm_d = din("Dm", (H, 128, 128))

    out_d = nc.dram_tensor("out", [TC, D], F32, kind="ExternalOutput").ap()
    tap_d = {}

    def tap(name, shape, dtype=F32):
        if name in taps:
            tap_d[name] = nc.dram_tensor("tap_" + name, list(shape), dtype,
                                         kind="ExternalOutput").ap()
        return tap_d.get(name)

    sx2_d = nc.dram_tensor("sx2", [128, NCH, TC], F32, kind="Internal").ap()
    ssp_d = nc.dram_tensor("ssp", [128, NCH, TC], F32, kind="Internal").ap()

    ctx = ExitStack()
    P = Plan()
    P.drain_gates = bool(taps)
    P.dram_keys = {"sx2", "ssp", "out", "tap1", "tap2", "tap3", "tap4",
                   "tap5", "tap6", "tap7"}
    perm = Scope(nc)

    R = {}

    def res(name):
        if name not in R:
            R[name] = Res(name)
        return R[name]

    ones_bf = perm.sb("ones_bf", [128, 128], BF16)
    ident_bf = perm.sb("ident_bf", [128, 128], BF16)
    ident_f = perm.sb("ident_f", [128, 128], F32)
    lnv = perm.sb("lnv", [128, 16 * 6], F32)
    b1s = perm.sb("b1s", [128, 64], F32)
    b2s = perm.sb("b2s", [128, 16], F32)
    maskb = perm.sb("maskb", [128, NGP], F32)
    mask1 = perm.sb("mask1", [1, NGP], F32)
    sqbf = perm.sb("sqbf", [128, 512], BF16)
    r_sqbf = res("sqbf")

    LVOFF = {"ln1w": 0, "ln1b": 16, "ln2w": 32, "ln2b": 48, "inb": 64, "outb": 80}

    def lncol(nm, m):
        return lnv[:, LVOFF[nm] + m:LVOFF[nm] + m + 1]

    r_const = res("const")
    P.op("gpsimd", lambda g: g.memset(ones_bf[:], 1.0), writes=[r_const])
    P.op("gpsimd", lambda g: g.memset(ident_bf[:], 0.0), writes=[r_const])
    P.op("gpsimd", lambda g: g.affine_select(out=ident_bf[:], in_=ident_bf[:],
         compare_op=ALU.not_equal, fill=1.0, base=0, pattern=[[-1, 128]],
         channel_multiplier=1), writes=[r_const])
    P.op("gpsimd", lambda g: g.memset(ident_f[:], 0.0), writes=[r_const])
    P.op("gpsimd", lambda g: g.affine_select(out=ident_f[:], in_=ident_f[:],
         compare_op=ALU.not_equal, fill=1.0, base=0, pattern=[[-1, 128]],
         channel_multiplier=1), writes=[r_const])

    r_vecs = res("lnvecs")
    for nm, dst in (("ln1w", lnv[:, 0:16]), ("ln1b", lnv[:, 16:32]),
                    ("ln2w", lnv[:, 32:48]), ("ln2b", lnv[:, 48:64]),
                    ("inb", lnv[:, 64:80]), ("outb", lnv[:, 80:96]),
                    ("b1v", b1s[:]), ("b2v", b2s[:]), ("maskp", mask1[:])):
        P.op("sync", lambda s, d=dst, src=d_in[nm]: s.dma_start(out=d, in_=src),
             writes=[r_vecs], dma="vecs")

    usc = Scope(nc)
    u_own = usc.sb("u_own", [128, H, L, NGO], BF16)
    u_prec = usc.sb("u_prec", [128, H, L, NGP], BF16)

    fsc = Scope(nc)
    mmP = [fsc.psum(f"mm{i}", [128, 512], F32) for i in range(2)]
    stP_s = fsc.psum("stP_s", [128, 128], F32)
    stP_q = fsc.psum("stP_q", [128, 128], F32)
    r_mmP = [res("mm0"), res("mm1")]
    r_stP = res("stP")

    r_maskb = res("maskb")
    P.op("vector", lambda v: v.tensor_copy(sqbf[0:1, 0:NGP], mask1[:]),
         reads=[r_vecs], writes=[r_maskb, r_sqbf])
    P.op("tensor", lambda t: t.matmul(stP_s[:, 0:NGP], ones_bf[0:1, :], sqbf[0:1, 0:NGP],
         start=True, stop=True, skip_group_check=True),
         reads=[r_maskb, r_const, r_sqbf], writes=[r_stP])
    P.op("vector", lambda v: v.tensor_copy(maskb[:], stP_s[:, 0:NGP]),
         reads=[r_stP], writes=[r_maskb])

    wr16 = [fsc.sb(f"wr16_{i}", [128, 16, 128], BF16) for i in range(4)]
    r_wr16 = [res(f"wr16_{i}") for i in range(4)]
    class WStream:

        def __init__(self, rbufs, rres, w_list, ahead=None):
            self.bufs, self.res, self.w = rbufs, rres, list(w_list)
            self.R = len(rbufs)
            self.ahead = self.R if ahead is None else ahead
            self.issued = 0

        def _issue(self):
            i = self.issued
            b, r = self.bufs[i % self.R], self.res[i % self.R]
            ap = self.w[i]
            nk = ap.shape[1]
            P.op("gpsimd", lambda g, d=b, ap=ap, nk=nk: g.dma_start(
                out=d[:, 0:nk, :], in_=ap), writes=[r], dma=r.name)
            self.issued += 1

        def get(self, i):
            while self.issued < min(i + self.ahead, len(self.w)):
                self._issue()
            return self.bufs[i % self.R], self.res[i % self.R]

    def stream_wcol(rbufs, rres, w_hbm, nk, m):
        ws = WStream(rbufs, rres, [w_hbm[m]])
        return ws.get(0)

    def proj_sweep(rbufs, rres, w_hbm, kdim, m_list, tiles, mov_fn, mov_res,
                   evac_fn):
        nk = kdim // 128
        ml = list(m_list)
        ws = WStream(rbufs, rres, [w_hbm[m] for m in ml])
        for mi, m in enumerate(ml):
            wsb, wres = ws.get(mi)
            for ti, (t0, tw) in enumerate(tiles):
                pb = (mi * len(tiles) + ti) % len(mmP)
                pst, prs = mmP[pb], r_mmP[pb]
                mres = mov_res(t0) if callable(mov_res) else mov_res
                for k in range(nk):
                    P.op("tensor", lambda t, k=k, pst=pst, t0=t0, tw=tw, wsb=wsb:
                         t.matmul(pst[:, 0:tw], wsb[:, k, :], mov_fn(k, t0, tw),
                                  start=(k == 0), stop=(k == nk - 1),
                                  skip_group_check=True),
                         reads=[wres] + mres,
                         writes=([prs] if k in (0, nk - 1) else []))
                evac_fn(m, pst, prs, ti)

    def stats_chunk(ps_sum, ps_sq, r_ps, xbf_ap, r_x, tw, first, last, sq0=0):
        P.op("scalar", lambda sc: sc.activation(sqbf[:, sq0:sq0 + tw], xbf_ap,
             AF.Square), reads=[r_x], writes=[r_sqbf])
        P.op("tensor", lambda t: t.matmul(ps_sum, ones_bf[:], xbf_ap,
             start=first, stop=last, skip_group_check=True),
             reads=[r_x, r_const], writes=([r_ps] if first else []))
        P.op("tensor", lambda t: t.matmul(ps_sq, ones_bf[:], sqbf[:, sq0:sq0 + tw],
             start=first, stop=last, skip_group_check=True),
             reads=[r_sqbf, r_const], writes=([r_ps] if last else []))

    def stats_finish(ps_sum, ps_sq, r_ps, mean_ap, inv_ap, t1_ap,
                     r_mean, r_inv, r_t1):
        P.op("vector", lambda v: v.tensor_scalar_mul(mean_ap, ps_sum, 1.0 / D),
             reads=[r_ps], writes=[r_mean])
        P.op("vector", lambda v: v.tensor_scalar_mul(t1_ap, ps_sq, 1.0 / D),
             reads=[r_ps], writes=[r_t1])
        P.op("vector", lambda v: v.tensor_mul(inv_ap, mean_ap, mean_ap),
             reads=[r_mean], writes=[r_inv])
        P.op("vector", lambda v: v.tensor_sub(t1_ap, t1_ap, inv_ap),
             reads=[r_inv], writes=[r_t1])
        P.op("vector", lambda v: v.tensor_scalar_add(t1_ap, t1_ap, EPS),
             writes=[r_t1])
        P.op("vector", lambda v: v.reciprocal(inv_ap, t1_ap),
             reads=[r_t1], writes=[r_inv])
        P.op("scalar", lambda s: s.activation(inv_ap, inv_ap, AF.Sqrt),
             writes=[r_inv])

    xt_sweep = fsc.sb("xt_sweep", [128, NCH, TC], BF16, side="right")
    xfm_bf = [fsc.sb(f"xfm_bf{i}", [128, NCH, 128], BF16, side="right")
              for i in range(3)]
    xfm_f = fsc.sb("xfm_f", [128, NCH, 128], F32, side="right")
    mean1 = fsc.sb("mean1", [128, 2, 128], F32, side="right")
    inv1 = fsc.sb("inv1", [128, 2, 128], F32, side="right")
    t1s = fsc.sb("t1s", [128, 128], F32, side="right")
    r_xt_sweep = [res("xt_sweep0"), res("xt_sweep1")]
    r_xtmb = [res("xtmb0"), res("xtmb1")]
    r_xfmb = [res(f"xfmb{i}") for i in range(3)]
    r_xtmf = res("xtmf")
    r_xfmf = res("xfmf")
    r_mean1 = [res("mean1a"), res("mean1b")]
    r_inv1 = [res("inv1a"), res("inv1b")]
    r_t1s = res("t1s")

    r_uo = res("u_own")
    r_up = res("u_prec")

    fr_count = [0]

    def frontend_tile(kind, src_ap, dst_xt, r_dst):
        i = fr_count[0] % 3
        fr_count[0] += 1
        if kind == "prec":
            xbf, r_xbf = xfm_bf[i], r_xfmb[i]
            P.op("gpsimd", lambda g, s=src_ap, d=xbf: g.dma_start(
                out=d[:], in_=s), writes=[r_xbf], dma=r_xbf.name)
        else:
            P.op("sync", lambda s, sa=src_ap: s.dma_start(
                out=xfm_f[:], in_=sa), writes=[r_xfmf], dma="xfmf")
            xbf, r_xbf = xfm_bf[i], r_xfmb[i]
            P.op("gpsimd", lambda g, s=src_ap, d=xbf: g.dma_start(
                out=d[:], in_=s), writes=[r_xbf], dma=r_xbf.name)
        pr = i % 2
        for c in range(NCH):
            stats_chunk(stP_s[:], stP_q[:], r_stP,
                        xbf[:, c, :], r_xbf, 128,
                        first=(c == 0), last=(c == NCH - 1))
        stats_finish(stP_s[:], stP_q[:], r_stP,
                     mean1[:, pr, :], inv1[:, pr, :], t1s[:],
                     r_mean1[pr], r_inv1[pr], r_t1s)
        src_t, r_srct = (xbf, r_xbf) if kind == "prec" else (xfm_f, r_xfmf)
        P.op("vector", lambda v, src_t=src_t, pr=pr: v.tensor_sub(
            src_t[:], src_t[:], bcast(mean1[:, pr, :], 1, NCH)),
            reads=[r_mean1[pr]], writes=[r_srct])
        P.op("vector", lambda v, src_t=src_t, pr=pr: v.tensor_mul(
            src_t[:], src_t[:], bcast(inv1[:, pr, :], 1, NCH)),
            reads=[r_inv1[pr]], writes=[r_srct])
        P.op("vector", lambda v, src_t=src_t: v.tensor_mul(
            src_t[:], src_t[:], bcast(lnv[:, 0:16], 2, 128)),
            reads=[r_vecs], writes=[r_srct])
        P.op("vector", lambda v, src_t=src_t, d=dst_xt: v.tensor_add(
            d[:], src_t[:], bcast(lnv[:, 16:32], 2, 128)),
            reads=[r_srct, r_vecs], writes=[r_dst])

    for sweep in range(4):
        own = (sweep == 3)
        for j in range(8):
            rh = r_xt_sweep[j // 4]
            if own:
                frontend_tile("own", xo_d[j],
                              xt_sweep[:, :, j * 128:(j + 1) * 128], rh)
            else:
                tile = sweep * 8 + j
                frontend_tile("prec", xp_d[tile],
                              xt_sweep[:, :, j * 128:(j + 1) * 128], rh)

        if own:
            def evac(m, pst, prs, ti):
                g0 = (ti * 512) // L
                src = pst[:, 0:512].rearrange("p (g i) -> p i g", i=L)
                P.op("scalar", lambda sc, src=src, m=m, g0=g0: sc.activation(
                    u_own[:, m, :, g0:g0 + 16], src, AF.Identity,
                    bias=lncol("inb", m), scale=1.0),
                    reads=[prs, r_vecs], writes=[r_uo])
        else:
            def evac(m, pst, prs, ti, sweep=sweep):
                g0 = (sweep * 1024 + ti * 512) // L
                src = pst[:, 0:512].rearrange("p (g i) -> p i g", i=L)
                P.op("scalar", lambda sc, src=src, m=m, g0=g0: sc.activation(
                    u_prec[:, m, :, g0:g0 + 16], src, AF.Identity,
                    bias=lncol("inb", m), scale=1.0),
                    reads=[prs, r_vecs], writes=[r_up])

        proj_sweep(wr16, r_wr16, in_w_d, D, range(H), [(0, 512), (512, 512)],
                   lambda k, t0, tw: xt_sweep[:, k, t0:t0 + tw],
                   lambda t0: [r_xt_sweep[t0 // 512]], evac)

    if tap("u_own", (128, H, L, NGO), BF16):
        P.op("sync", lambda s: s.dma_start(out=tap_d["u_own"], in_=u_own[:]),
             reads=[r_uo], dma="tap1")
    if tap("u_prec", (128, H, L, NGP), BF16):
        P.op("sync", lambda s: s.dma_start(out=tap_d["u_prec"], in_=u_prec[:]),
             reads=[r_up], dma="tap2")
    if tap("xt_own", (128, NCH, TC), BF16):
        P.op("sync", lambda s: s.dma_start(out=tap_d["xt_own"], in_=xt_sweep[:]),
             reads=r_xt_sweep, dma="tap3")

    if stage == "uproj":
        P.op("gpsimd", lambda g: g.dma_start(
            out=out_d[0:128, 0:512],
            in_=u_own[:, 0, :, :].rearrange("p i g -> p (i g)")[:, 0:512]),
            reads=[r_uo], dma="out")
        fsc.close()
        usc.close()
        finish(nc, ctx, P, perm)
        return nc, tap_d

    P.gate()
    fsc.close()
    ysc = Scope(nc)
    y_sb = ysc.sb("y_sb", [128, H, TC], BF16, side="right")
    r_y = res("y_sb")
    ssc = Scope(nc)
    phSB = [ssc.psum(f"phS{i}", [128, TC], F32) for i in range(2)]
    pvy = [ssc.psum(f"pvy{i}", [128, 512], F32) for i in range(2)]
    paggB = [ssc.psum(f"pagg{i}", [128, 512], F32) for i in range(2)]
    r_phSB = [res("phS0"), res("phS1")]
    r_pvy = [res("pvy0"), res("pvy1")]
    r_paggB = [res("pagg0"), res("pagg1")]
    trctr = [0]

    abcd = [ssc.sb("abcd0", [128, 4 * 128], BF16)] * 2
    PtBB = [ssc.sb(f"Pt{i}", [128, (L + 1) * 128], BF16) for i in range(2)]
    PPtB1 = ssc.sb("PPt", [128, (L + 1) * 128], BF16)
    trs1 = ssc.sb("trs", [128, 5 * 128], BF16)
    v_pre = [ssc.sb(f"v_pre{i}", [128, L * NGP], BF16) for i in range(2)]
    v_ownb = ssc.sb("v_ownb", [128, TC], BF16)
    h_bf = ssc.sb("h_bf", [128, TC], BF16)
    ntop = [ssc.sb(f"ntop{i}", [128, 128], BF16) for i in range(2)]
    r_ntop = [res("ntop0"), res("ntop1")]
    S_pre = ssc.sb("S_pre", [128, NGP], BF16)
    S_ext = ssc.sb("S_ext", [128, L], BF16)
    Hprev = ssc.sb("Hprev", [128, NGO], BF16)
    Zc = ssc.sb("Zc", [128, 4], BF16)
    r_abcd = [res("abcd0")] * 2
    r_PtB = [res("Pt0"), res("Pt1")]
    r_PPt1 = res("PPt")
    r_trs1 = {nm: res("trs_" + nm) for nm in ("Bt", "Ct", "Dt", "PPQ", "A1024n")}
    r_vpre = [res("v_pre0"), res("v_pre1")]
    r_vown = res("v_ownb")
    r_hbf = res("h_bf")
    r_Spre = res("S_pre")
    r_Sext = res("S_ext")
    r_Hprev = res("Hprev")
    r_Zc = res("Zc")

    TRS = {"Bt": 0, "Ct": 1, "Dt": 2, "PPQ": 3, "A1024n": 4}

    def ptk(buf, k):
        return buf[:, k * 128:(k + 1) * 128]

    def transpose_to(dst_ap, dst_res, src_ap, src_res, slot=None):
        sl = trctr[0] % 2
        trctr[0] += 1
        P.op("tensor", lambda t, s=src_ap, sl=sl: t.matmul(
            pvy[sl][:, 0:128], s, ident_bf[:], start=True, stop=True,
            skip_group_check=True),
            reads=[src_res, r_const], writes=[r_pvy[sl]])
        P.op("scalar", lambda sc, d=dst_ap, sl=sl: sc.copy(
            d, pvy[sl][:, 0:128]), reads=[r_pvy[sl]], writes=[dst_res])

    def emit_head(h):
        hb = h % 2
        A_sb = abcd[hb][:, 0:128]
        B_sb = abcd[hb][:, 128:256]
        C_sb = abcd[hb][:, 256:384]
        D_sb = abcd[hb][:, 384:512]
        myv, r_myv = v_pre[hb], r_vpre[hb]
        myph, r_myph = phSB[hb], r_phSB[hb]
        pagg, r_pagg = paggB[hb], r_paggB[hb]
        myntop, r_myntop = ntop[hb], r_ntop[hb]
        PtB, r_Pt = PtBB[hb], r_PtB[hb]
        PPtB, r_PPt = PPtB1, r_PPt1
        trs, r_trs = trs1, r_trs1

        def trsl(nm):
            o = TRS[nm] * 128
            return trs[:, o:o + 128]

        def chain_doubling(buf, r_buf, base_nat, r_base):
            P.op("scalar", lambda sc, b=base_nat: sc.copy(myntop[:], b),
                 reads=[r_base], writes=[r_myntop])
            P.op("tensor", lambda t, b=base_nat: t.matmul(
                pvy[0][:, 0:128], b, ident_bf[:], start=True, stop=True,
                skip_group_check=True),
                reads=[r_base, r_const], writes=[r_pvy[0]])
            P.op("scalar", lambda sc, buf=buf: sc.copy(
                ptk(buf, 1), pvy[0][:, 0:128]), reads=[r_pvy[0]], writes=[r_buf])
            kk = 1
            while kk < L:
                nn = min(kk, L - kk)
                done = 0
                while done < nn:
                    take = min(4, nn - done)
                    pb = (done // 4) % 2
                    P.op("tensor", lambda t, done=done, take=take, pb=pb,
                         buf=buf:
                         t.matmul(pvy[pb][:, 0:take * 128], myntop[:],
                                  buf[:, (1 + done) * 128:(1 + done + take) * 128],
                                  start=True, stop=True, skip_group_check=True),
                         reads=[r_myntop, r_buf], writes=[r_pvy[pb]])
                    P.op("scalar", lambda sc, kk=kk, done=done, take=take, pb=pb,
                         buf=buf: sc.copy(
                             buf[:, (kk + 1 + done) * 128:(kk + 1 + done + take) * 128],
                             pvy[pb][:, 0:take * 128]),
                         reads=[r_pvy[pb]], writes=[r_buf])
                    done += take
                if kk + nn < L:
                    P.op("tensor", lambda t, buf=buf, nn=nn: t.matmul(
                        pvy[1][:, 0:128], ptk(buf, nn), myntop[:],
                        start=True, stop=True, skip_group_check=True),
                        reads=[r_buf, r_myntop], writes=[r_pvy[1]])
                    P.op("scalar", lambda sc: sc.copy(myntop[:], pvy[1][:, 0:128]),
                         reads=[r_pvy[1]], writes=[r_myntop])
                kk += nn

        for idx, src in ((0, Am_d), (1, Bm_d), (2, Cm_d), (3, Dm_d)):
            P.op("gpsimd", lambda g, idx=idx, src=src: g.dma_start(
                out=abcd[hb][:, idx * 128:(idx + 1) * 128], in_=src[h, :, :]),
                writes=[r_abcd[hb]], dma=r_abcd[hb].name)

        transpose_to(trsl("Bt"), r_trs["Bt"], B_sb, r_abcd[hb])
        transpose_to(trsl("Ct"), r_trs["Ct"], C_sb, r_abcd[hb])
        transpose_to(trsl("Dt"), r_trs["Dt"], D_sb, r_abcd[hb])

        P.op("vector", lambda v: v.tensor_copy(ptk(PtB, 0), ident_bf[:]),
             reads=[r_const], writes=[r_Pt])
        chain_doubling(PtB, r_Pt, A_sb, r_abcd[hb])

        uof = u_own[:, h, :, :].rearrange("p i g -> p (i g)")
        for half in range(2):
            pb = half % 2
            P.op("tensor", lambda t, half=half, pb=pb, uof=uof: t.matmul(
                pvy[pb][:], trsl("Bt"), uof[:, half * 512:(half + 1) * 512],
                start=True, stop=True, skip_group_check=True),
                reads=[r_trs["Bt"], r_uo], writes=[r_pvy[pb]])
            P.op("scalar", lambda sc, half=half, pb=pb: sc.copy(
                v_ownb[:, half * 512:(half + 1) * 512], pvy[pb][:]),
                reads=[r_pvy[pb]], writes=[r_vown])

        upf = u_prec[:, h, :, :].rearrange("p i g -> p (i g)")
        for j in range(6):
            pb = j % 2
            P.op("tensor", lambda t, j=j, pb=pb: t.matmul(
                pvy[pb][:], trsl("Bt"), upf[:, j * 512:(j + 1) * 512],
                start=True, stop=True, skip_group_check=True),
                reads=[r_trs["Bt"], r_up], writes=[r_pvy[pb]])
            P.op("scalar", lambda sc, j=j, pb=pb, myv=myv: sc.copy(
                myv[:, j * 512:(j + 1) * 512], pvy[pb][:]),
                reads=[r_pvy[pb]], writes=[r_myv])

        for i in range(L):
            P.op("tensor", lambda t, i=i: t.matmul(
                pagg[:, 0:NGP], ptk(PtB, L - 1 - i),
                myv[:, i * NGP:(i + 1) * NGP],
                start=(i == 0), stop=(i == L - 1), skip_group_check=True),
                reads=[r_Pt, r_myv], writes=([r_pagg] if i in (0, L - 1) else []))
        P.op("vector", lambda v: v.tensor_mul(S_pre[:], pagg[:, 0:NGP], maskb[:]),
             reads=[r_pagg, r_maskb], writes=[r_Spre])

        first_conv = True
        for k in range(L):
            c0 = k * NGO
            pieces = [(c0, 512), (512, TC)] if c0 < 512 else [(c0, TC)]
            for (a, bnd) in pieces:
                if a >= bnd:
                    continue
                last = (k == L - 1 and bnd == TC)
                P.op("tensor", lambda t, k=k, a=a, bnd=bnd, myph=myph: t.matmul(
                    myph[:, a:bnd], ptk(PtB, k),
                    v_ownb[:, a - k * NGO:bnd - k * NGO],
                    start=(k == 0), stop=False, skip_group_check=True),
                    reads=[r_Pt, r_vown],
                    writes=([r_myph] if (first_conv or last) else []))
                first_conv = False
        P.op("vector", lambda v, myph=myph: v.tensor_copy(
            S_ext[:, 1:L], myph[:, 992:1023]), reads=[r_myph], writes=[r_Sext])

        P.op("tensor", lambda t: t.matmul(pvy[0][:, 0:128], ptk(PtB, L // 2),
             myntop[:], start=True, stop=True, skip_group_check=True),
             reads=[r_Pt, r_myntop], writes=[r_pvy[0]])
        P.op("scalar", lambda sc: sc.copy(trsl("PPQ"), pvy[0][:, 0:128]),
             reads=[r_pvy[0]], writes=[r_trs["PPQ"]])
        P.op("vector", lambda v: v.tensor_copy(ptk(PPtB, 0), ident_bf[:]),
             reads=[r_const], writes=[r_PPt])
        chain_doubling(PPtB, r_PPt, trsl("PPQ"), r_trs["PPQ"])

        for m in range(L):
            P.op("tensor", lambda t, m=m: t.matmul(
                pagg[:, 256:259], ptk(PPtB, m), S_pre[:, 31 - m::L],
                start=(m == 0), stop=(m == L - 1), skip_group_check=True),
                reads=[r_PPt, r_Spre], writes=([r_pagg] if m in (0, L - 1) else []))
        P.op("vector", lambda v: v.tensor_copy(Zc[:, 0:3], pagg[:, 256:259]),
             reads=[r_pagg], writes=[r_Zc])

        P.op("tensor", lambda t: t.matmul(pvy[1][:, 0:128], ptk(PPtB, L // 2),
             myntop[:], start=True, stop=True, skip_group_check=True),
             reads=[r_PPt, r_myntop], writes=[r_pvy[1]])
        P.op("scalar", lambda sc: sc.copy(trsl("A1024n"), pvy[1][:, 0:128]),
             reads=[r_pvy[1]], writes=[r_trs["A1024n"]])
        P.op("tensor", lambda t: t.matmul(pagg[:, 300:428], trsl("A1024n"),
             ptk(PPtB, L), start=True, stop=True, skip_group_check=True),
             reads=[r_trs["A1024n"], r_PPt], writes=[r_pagg])
        P.op("vector", lambda v: v.tensor_copy(trsl("PPQ"), pagg[:, 300:428]),
             reads=[r_pagg], writes=[r_trs["PPQ"]])
        P.op("tensor", lambda t: t.matmul(pagg[:, 448:449], trsl("PPQ"),
             Zc[:, 0:1], start=True, stop=False, skip_group_check=True),
             reads=[r_trs["PPQ"], r_Zc], writes=[r_pagg])
        P.op("tensor", lambda t: t.matmul(pagg[:, 448:449], ptk(PPtB, L),
             Zc[:, 1:2], start=False, stop=False, skip_group_check=True),
             reads=[r_PPt, r_Zc])
        P.op("tensor", lambda t: t.matmul(pagg[:, 448:449], ident_bf[:],
             Zc[:, 2:3], start=False, stop=True, skip_group_check=True),
             reads=[r_const, r_Zc], writes=[r_pagg])
        P.op("vector", lambda v: v.tensor_copy(S_ext[:, 0:1], pagg[:, 448:449]),
             reads=[r_pagg], writes=[r_Sext])

        for m in range(NGO):
            P.op("tensor", lambda t, m=m: t.matmul(
                pagg[:, 256 + m:256 + NGO], ptk(PPtB, m), S_ext[:, 0:NGO - m],
                start=(m == 0), stop=(m == NGO - 1), skip_group_check=True),
                reads=[r_PPt, r_Sext], writes=([r_pagg] if m in (0, NGO - 1) else []))
        P.op("vector", lambda v: v.tensor_copy(Hprev[:], pagg[:, 256:256 + NGO]),
             reads=[r_pagg], writes=[r_Hprev])

        for i in range(L):
            first = (i == 0)
            last = (i == L - 1)
            P.op("tensor", lambda t, i=i, myph=myph: t.matmul(
                myph[:, i * NGO:(i + 1) * NGO], ptk(PtB, i + 1), Hprev[:],
                start=False, stop=last, skip_group_check=True),
                reads=[r_Pt, r_Hprev],
                writes=([r_myph] if (first or last) else []))

        for half in range(2):
            P.op("scalar", lambda sc, half=half, myph=myph: sc.copy(
                h_bf[:, half * 512:(half + 1) * 512],
                myph[:, half * 512:(half + 1) * 512]),
                reads=[r_myph], writes=[r_hbf])
        yv = y_sb[:, h, :].rearrange("p (g i) -> p i g", i=L)
        for half in range(2):
            pb = half % 2
            P.op("tensor", lambda t, half=half, pb=pb: t.matmul(
                pvy[pb][:], trsl("Ct"), h_bf[:, half * 512:(half + 1) * 512],
                start=True, stop=False, skip_group_check=True),
                reads=[r_trs["Ct"], r_hbf], writes=[r_pvy[pb]])
            P.op("tensor", lambda t, half=half, pb=pb, uof=uof: t.matmul(
                pvy[pb][:], trsl("Dt"), uof[:, half * 512:(half + 1) * 512],
                start=False, stop=True, skip_group_check=True),
                reads=[r_trs["Dt"], r_uo], writes=[r_pvy[pb]])
            P.op("scalar", lambda sc, half=half, pb=pb, yv=yv: sc.copy(
                yv[:, half * 16:(half + 1) * 16, :],
                pvy[pb][:].rearrange("p (i g) -> p i g", g=NGO)),
                reads=[r_pvy[pb]], writes=[r_y])

    for h in range(H):
        emit_head(h)

    if tap("y", (128, H, TC), BF16):
        P.op("sync", lambda s: s.dma_start(out=tap_d["y"], in_=y_sb[:]),
             reads=[r_y], dma="tap4")

    if stage == "ssm":
        P.op("gpsimd", lambda g: g.dma_start(
            out=out_d[0:128, 0:512], in_=y_sb[:, 0, 0:512]),
            reads=[r_y], dma="out")
        ssc.close()
        ysc.close()
        usc.close()
        finish(nc, ctx, P, perm)
        return nc, tap_d

    P.gate()
    ssc.close()
    usc.close()

    tsc = Scope(nc)
    mm2 = [tsc.psum(f"mm2_{i}", [128, 512], F32) for i in range(2)]
    sums = tsc.psum("sums", [128, 2, TC], F32)
    trfT = tsc.psum("trfT", [128, 128], F32)
    trbT = tsc.psum("trbT", [128, 128], BF16)
    r_mm2 = [res("mm2_0"), res("mm2_1")]
    r_sums2 = res("sums2")
    r_trfT = res("trfT")
    r_trbT = res("trbT")

    meanF = tsc.sb("meanF", [128, TC], F32)
    invF = tsc.sb("invF", [128, TC], F32)
    x2c = tsc.sb("x2c", [128, TC], F32)
    x2bfc = tsc.sb("x2bfc", [128, TC], BF16)
    wrT = [tsc.sb(f"wrT{i}", [128, 16, 128], BF16) for i in range(3)]
    xr = [tsc.sb(f"xr{i}", [128, TC], F32) for i in range(2)]
    r_meanF, r_invF = res("meanF"), res("invF")
    r_x2c, r_x2bfc = res("x2c"), res("x2bfc")
    r_wrT = [res(f"wrT{i}") for i in range(3)]
    r_xr = [res("xr0"), res("xr1")]
    r_sx2d = res("sx2_d")
    r_sspd = res("ssp_d")

    def stats_full(m, last_m, src_bf_ap, r_src):
        for half in range(2):
            t0 = half * 512
            stats_chunk(sums[:, 0, t0:t0 + 512], sums[:, 1, t0:t0 + 512],
                        r_sums2, src_bf_ap[:, t0:t0 + 512], r_src, 512,
                        first=(m == 0), last=(m == last_m), sq0=0)

    def stats_full_finish():
        for half in range(2):
            t0 = half * 512
            stats_finish(sums[:, 0, t0:t0 + 512], sums[:, 1, t0:t0 + 512],
                         r_sums2, meanF[:, t0:t0 + 512], invF[:, t0:t0 + 512],
                         x2c[:, t0:t0 + 512], r_meanF, r_invF, r_x2c)

    def evac_op(m, pst, prs, ti):
        xrb, r_xrb = xr[m % 2], r_xr[m % 2]
        if ti == 0:
            P.op("sync", lambda s, m=m, d=xrb: s.dma_start(
                out=d[:].rearrange("p (a t) -> p a t", a=8), in_=xo_d[:, :, m, :].rearrange("a p t -> p a t")), reads=[],
                writes=[r_xrb], dma=r_xrb.name)
        t0 = ti * 512
        P.op("vector", lambda v, m=m, t0=t0, pst=pst, xrb=xrb: v.scalar_tensor_tensor(
            x2c[:, t0:t0 + 512], pst[:, 0:512], lncol("outb", m),
            xrb[:, t0:t0 + 512], ALU.add, ALU.add),
            reads=[prs, r_vecs, r_xrb], writes=[r_x2c])
        if ti == 1:
            P.op("scalar", lambda sc: sc.copy(x2bfc[:], x2c[:]),
                 reads=[r_x2c], writes=[r_x2bfc])
            stats_full(m, NCH - 1, x2bfc, r_x2bfc)
            P.op("sync", lambda s, m=m: s.dma_start(out=sx2_d[:, m, :],
                 in_=x2c[:]), reads=[r_x2c], writes=[r_sx2d], dma="sx2")

    mmP, r_mmP = mm2, r_mm2
    proj_sweep(wrT, r_wrT, out_w_d, D, range(NCH), [(0, 512), (512, 512)],
               lambda k, t0, tw: y_sb[:, k, t0:t0 + tw], [r_y], evac_op)
    stats_full_finish()

    if tap("x2", (128, NCH, TC)):
        P.op("sync", lambda s: s.dma_start(out=tap_d["x2"], in_=sx2_d[:]),
             reads=[r_sx2d], dma="tap5")

    P.gate()
    ysc.close()
    t1sc = Scope(nc)
    xt2 = t1sc.sb("xt2", [128, NCH, TC], BF16, side="right")
    r_xt2 = res("xt2")
    lnslots = [(xr[0], r_xr[0], "sync"), (xr[1], r_xr[1], "sync"),
               (x2c, r_x2c, "sync")]
    for m in range(NCH):
        xrb, r_xrb, qeng = lnslots[m % 3]
        P.op(qeng, lambda s, m=m, d=xrb: s.dma_start(
            out=d[:], in_=sx2_d[:, m, :]), reads=[r_sx2d],
            writes=[r_xrb], dma=r_xrb.name)
        P.op("vector", lambda v, xrb=xrb: v.tensor_sub(
            xrb[:], xrb[:], meanF[:]), reads=[r_meanF], writes=[r_xrb])
        P.op("vector", lambda v, xrb=xrb: v.tensor_mul(
            xrb[:], xrb[:], invF[:]), reads=[r_invF], writes=[r_xrb])
        P.op("scalar", lambda sc, m=m, xrb=xrb: sc.activation(
            xt2[:, m, :], xrb[:], AF.Identity,
            bias=lncol("ln2b", m), scale=lncol("ln2w", m)),
            reads=[r_xrb, r_vecs], writes=[r_xt2])
    if tap("xt2", (128, NCH, TC), BF16):
        P.op("sync", lambda s: s.dma_start(out=tap_d["xt2"], in_=xt2[:]),
             reads=[r_xt2], dma="tap6")

    h_ffn = t1sc.sb("h_ffn", [128, NH, TC], BF16)
    r_hffn = res("h_ffn")

    def evac_f1(m, pst, prs, ti):
        t0 = ti * 512
        P.op("scalar", lambda s, m=m, t0=t0, pst=pst: s.activation(
            h_ffn[:, m, t0:t0 + 512], pst[:, 0:512], AF.Gelu,
            bias=b1s[:, m:m + 1], scale=1.0),
            reads=[prs, r_vecs], writes=[r_hffn])

    proj_sweep(wrT, r_wrT, w1_d, D, range(NH), [(0, 512), (512, 512)],
               lambda k, t0, tw: xt2[:, k, t0:t0 + tw], [r_xt2], evac_f1)

    P.gate()
    t1sc.close_right()
    t2sc = Scope(nc)
    wr64 = [t2sc.sb(f"wr64_{i}", [128, 64, 128], BF16, side="right")
            for i in range(2)]
    r_wr64 = [res("wr64_0"), res("wr64_1")]
    xr2 = [t2sc.sb(f"xr2{i}", [128, TC], F32) for i in range(2)]
    r_xr2 = [res("xr2_0"), res("xr2_1")]

    def evac_f2(m, pst, prs, ti):
        xrb, r_xrb = xr[m % 2], r_xr[m % 2]
        xob, r_xob = xr2[m % 2], r_xr2[m % 2]
        if ti == 0:
            P.op("sync", lambda s, m=m, d=xrb: s.dma_start(
                out=d[:], in_=sx2_d[:, m, :]), reads=[r_sx2d],
                writes=[r_xrb], dma=r_xrb.name)
            P.op("sync", lambda s, m=m, d=xob: s.dma_start(
                out=d[:].rearrange("p (a t) -> p a t", a=8), in_=xo_d[:, :, m, :].rearrange("a p t -> p a t")), reads=[],
                writes=[r_xob], dma=r_xob.name)
        t0 = ti * 512
        P.op("vector", lambda v, m=m, t0=t0, pst=pst, xrb=xrb: v.scalar_tensor_tensor(
            x2c[:, t0:t0 + 512], pst[:, 0:512], b2s[:, m:m + 1],
            xrb[:, t0:t0 + 512], ALU.add, ALU.add),
            reads=[prs, r_vecs, r_xrb], writes=[r_x2c])
        P.op("vector", lambda v, t0=t0, xob=xob: v.tensor_add(
            x2c[:, t0:t0 + 512], x2c[:, t0:t0 + 512], xob[:, t0:t0 + 512]),
            reads=[r_xob], writes=[r_x2c])
        if ti == 1:
            P.op("scalar", lambda sc: sc.copy(x2bfc[:], x2c[:]),
                 reads=[r_x2c], writes=[r_x2bfc])
            stats_full(m, NCH - 1, x2bfc, r_x2bfc)
            P.op("sync", lambda s, m=m: s.dma_start(out=ssp_d[:, m, :],
                 in_=x2c[:]), reads=[r_x2c], writes=[r_sspd], dma="ssp")

    proj_sweep(wr64, r_wr64, w2_d, HID, range(NCH), [(0, 512), (512, 512)],
               lambda k, t0, tw: h_ffn[:, k, t0:t0 + tw], [r_hffn], evac_f2)
    stats_full_finish()

    P.gate()
    t2sc.close()
    t1sc.close()
    t3sc = Scope(nc)
    xA = t3sc.sb("xA", [128, NCH, TC], F32)
    he = t3sc.sb("he", [128, NSH, TC], BF16)
    Cb = t3sc.sb("Cb", [128, TC], BF16)
    rsc = t3sc.sb("rsc", [128, 8], F32)
    rs2 = t3sc.sb("rs2", [128, 12], F32)
    combbf = t3sc.sb("combbf", [128, 8], BF16)
    combT = t3sc.sb("combT", [8, TC], BF16)
    selE = t3sc.sb("selE", [8, 8 * 128], BF16)
    wrb = t3sc.sb("wrb", [128, NCH, 8], BF16)
    tsub = Scope(nc)
    t1b = tsub.sb("t1b", [128, TC], BF16)
    t2b = tsub.sb("t2b", [128, TC], BF16)
    xAbf = t3sc.sb("xAbf", [128, NCH, TC], BF16, side="right")
    wrM = [t3sc.sb(f"wrM{i}", [128, 22, 128], BF16, side="right")
           for i in range(3)]
    r_xAp = [res("xA_ev"), res("xA_od")]
    r_he = res("he")
    r_Cb = res("Cb")
    r_t1b, r_t2b = res("t1b"), res("t2b")
    r_rsc, r_rs2 = res("rsc"), res("rs2")
    r_combbf, r_combT = res("combbf"), res("combT")
    r_selE = res("selE")
    r_wrb = res("wrb")
    r_xAbf = res("xAbf")
    r_wrM = [res(f"wrM{i}") for i in range(3)]

    P.op("gpsimd", lambda g: g.memset(selE[:], 0.0), writes=[r_selE])
    P.op("gpsimd", lambda g: g.affine_select(
        out=selE[:].rearrange("p (e i) -> p e i", i=128),
        in_=selE[:].rearrange("p (e i) -> p e i", i=128),
        compare_op=ALU.not_equal, fill=1.0, base=0,
        pattern=[[-1, E], [0, 128]], channel_multiplier=1), writes=[r_selE])
    P.op("gpsimd", lambda g: g.dma_start(
        out=wrb[:], in_=wr_d.rearrange("(k p) e -> p k e", p=128)),
        writes=[r_wrb], dma="wrb")

    for m in range(NCH):
        xrb, r_xrb, qeng = lnslots[m % 3]
        P.op(qeng, lambda s, m=m, d=xrb: s.dma_start(
            out=d[:], in_=ssp_d[:, m, :]), reads=[r_sspd],
            writes=[r_xrb], dma=r_xrb.name)
        P.op("vector", lambda v, xrb=xrb: v.tensor_sub(
            xrb[:], xrb[:], meanF[:]), reads=[r_meanF], writes=[r_xrb])
        P.op("vector", lambda v, m=m, xrb=xrb: v.tensor_mul(
            xA[:, m, :], xrb[:], invF[:]), reads=[r_xrb, r_invF],
            writes=[r_xAp[m % 2]])
        P.op("scalar", lambda sc, m=m: sc.copy(xAbf[:, m, :], xA[:, m, :]),
             reads=[r_xAp[m % 2]], writes=[r_xAbf])
    if tap("xA", (128, NCH, TC)):
        P.op("sync", lambda s: s.dma_start(out=tap_d["xA"], in_=xA[:]),
             reads=[r_xAp[0], r_xAp[1]], dma="tap7")

    for tt in range(8):
        tt0 = tt * 128
        for k in range(NCH):
            P.op("tensor", lambda t, k=k, tt0=tt0: t.matmul(
                trfT[:, 0:8], xAbf[:, k, tt0:tt0 + 128], wrb[:, k, :],
                start=(k == 0), stop=(k == NCH - 1), skip_group_check=True),
                reads=[r_xAbf, r_wrb],
                writes=([r_trfT] if k in (0, NCH - 1) else []))
        P.op("vector", lambda v: v.tensor_copy(rsc[:], trfT[:, 0:8]),
             reads=[r_trfT], writes=[r_rsc])
        P.op("vector", lambda v: v.reduce_max(rs2[:, 0:1], rsc[:], axis=AX.X,
             negate=True), reads=[r_rsc], writes=[r_rs2])
        P.op("scalar", lambda s: s.activation(rsc[:], rsc[:], AF.Exp,
             bias=rs2[:, 0:1], scale=1.0), reads=[r_rs2], writes=[r_rsc])
        P.op("vector", lambda v: v.reduce_sum(rs2[:, 1:2], rsc[:], axis=AX.X),
             reads=[r_rsc], writes=[r_rs2])
        P.op("vector", lambda v: v.reciprocal(rs2[:, 2:3], rs2[:, 1:2]),
             writes=[r_rs2])
        P.op("vector", lambda v: v.tensor_scalar_mul(rsc[:], rsc[:],
             rs2[:, 2:3]), reads=[r_rs2], writes=[r_rsc])
        P.op("vector", lambda v: v.reduce_max(rs2[:, 3:4], rsc[:], axis=AX.X),
             reads=[r_rsc], writes=[r_rs2])
        P.op("vector", lambda v: v.tensor_single_scalar(rs2[:, 4:12], rsc[:],
             rs2[:, 3:4], ALU.is_ge), reads=[r_rsc], writes=[r_rs2])
        P.op("vector", lambda v: v.scalar_tensor_tensor(rs2[:, 4:12],
             rs2[:, 4:12], -1e30, rsc[:], ALU.mult, ALU.add),
             reads=[r_rsc], writes=[r_rs2])
        P.op("vector", lambda v: v.reduce_max(rs2[:, 3:4], rs2[:, 4:12],
             axis=AX.X), writes=[r_rs2])
        P.op("vector", lambda v: v.tensor_single_scalar(rs2[:, 4:12], rsc[:],
             rs2[:, 3:4], ALU.is_ge), reads=[r_rsc], writes=[r_rs2])
        P.op("vector", lambda v: v.tensor_mul(combbf[:], rsc[:], rs2[:, 4:12]),
             reads=[r_rsc, r_rs2], writes=[r_combbf])
        P.op("tensor", lambda t: t.transpose(
            trbT[0:8, :], combbf[:], ident_bf[:]),
            reads=[r_combbf, r_const], writes=[r_trbT])
        P.op("vector", lambda v, tt0=tt0: v.tensor_copy(
            combT[:, tt0:tt0 + 128], trbT[0:8, :]),
            reads=[r_trbT], writes=[r_combT])

    def gateup_sweep(wg_hbm, wu_hbm, n_m, kdim, cb_scale, dst, r_dst):
        nk = kdim // 128
        b0 = moe_base[0]
        for m in range(n_m):
            wg, r_wg = ws_moe.get(b0 + 2 * m)
            wu, r_wu = ws_moe.get(b0 + 2 * m + 1)
            for ti in range(2):
                t0 = ti * 512
                pg, r_pg = mm2[0], r_mm2[0]
                pu, r_pu = mm2[1], r_mm2[1]
                for k in range(nk):
                    P.op("tensor", lambda t, k=k, t0=t0, wg=wg: t.matmul(
                        pg[:, 0:512], wg[:, k, :], xAbf[:, k, t0:t0 + 512],
                        start=(k == 0), stop=(k == nk - 1),
                        skip_group_check=True),
                        reads=[r_wg, r_xAbf],
                        writes=([r_pg] if k in (0, nk - 1) else []))
                P.op("scalar", lambda s, t0=t0: s.activation(
                    t1b[:, t0:t0 + 512], pg[:, 0:512], AF.Silu),
                    reads=[r_pg], writes=[r_t1b])
                for k in range(nk):
                    P.op("tensor", lambda t, k=k, t0=t0, wu=wu: t.matmul(
                        pu[:, 0:512], wu[:, k, :], xAbf[:, k, t0:t0 + 512],
                        start=(k == 0), stop=(k == nk - 1),
                        skip_group_check=True),
                        reads=[r_wu, r_xAbf],
                        writes=([r_pu] if k in (0, nk - 1) else []))
                if cb_scale:
                    P.op("vector", lambda v, t0=t0: v.tensor_mul(
                        t2b[:, t0:t0 + 512], pu[:, 0:512], Cb[:, t0:t0 + 512]),
                        reads=[r_pu, r_Cb], writes=[r_t2b])
                else:
                    P.op("vector", lambda v, t0=t0: v.tensor_copy(
                        t2b[:, t0:t0 + 512], pu[:, 0:512]),
                        reads=[r_pu], writes=[r_t2b])
                P.op("vector", lambda v, m=m, t0=t0, dst=dst: v.tensor_mul(
                    dst[:, m, t0:t0 + 512], t1b[:, t0:t0 + 512],
                    t2b[:, t0:t0 + 512]),
                    reads=[r_t1b, r_t2b], writes=[r_dst])
        moe_base[0] += 2 * n_m

    def down_sweep(wd_hbm, nk, src, r_src):
        b0 = moe_base[0]
        for m in range(NCH):
            wd, r_wd = ws_moe.get(b0 + m)
            for ti in range(2):
                t0 = ti * 512
                pb = ti % 2
                pst, prs = mm2[pb], r_mm2[pb]
                for k in range(nk):
                    P.op("tensor", lambda t, k=k, t0=t0, wd=wd, src=src,
                         pst=pst: t.matmul(
                        pst[:, 0:512], wd[:, k, :], src[:, k, t0:t0 + 512],
                        start=(k == 0), stop=(k == nk - 1),
                        skip_group_check=True),
                        reads=[r_wd, r_src],
                        writes=([prs] if k in (0, nk - 1) else []))
                P.op("vector", lambda v, m=m, t0=t0, pst=pst: v.tensor_add(
                    xA[:, m, t0:t0 + 512], xA[:, m, t0:t0 + 512],
                    pst[:, 0:512]), reads=[prs], writes=[r_xAp[m % 2]])
        moe_base[0] += NCH

    moe_wl = []
    for m in range(NSH):
        moe_wl.append(wsg_d[m])
        moe_wl.append(wsu_d[m])
    moe_wl += [wsd_d[m] for m in range(NCH)]
    for e in range(E):
        for m in range(NEH):
            moe_wl.append(weg_d[e][m])
            moe_wl.append(weu_d[e][m])
        moe_wl += [wed_d[e][m] for m in range(NCH)]
    ws_moe = WStream(wrM, r_wrM, moe_wl, ahead=len(wrM) - 1)
    moe_base = [0]

    gateup_sweep(wsg_d, wsu_d, NSH, D, False, he, r_he)
    down_sweep(wsd_d, NSH, he, r_he)

    for e in range(E):
        for half in range(2):
            t0 = half * 512
            pb = half % 2
            P.op("tensor", lambda t, e=e, t0=t0, pb=pb: t.matmul(
                mm2[pb][:, 0:512], selE[:, e * 128:(e + 1) * 128],
                combT[:, t0:t0 + 512], start=True, stop=True,
                skip_group_check=True),
                reads=[r_selE, r_combT], writes=[r_mm2[pb]])
            P.op("vector", lambda v, t0=t0, pb=pb: v.tensor_copy(
                Cb[:, t0:t0 + 512], mm2[pb][:, 0:512]),
                reads=[r_mm2[pb]], writes=[r_Cb])
        gateup_sweep(weg_d[e], weu_d[e], NEH, D, True, he, r_he)
        down_sweep(wed_d[e], NEH, he, r_he)

    tsub.close()
    for m in range(NCH):
        P.op("scalar", lambda sc, m=m: sc.copy(x2bfc[:], xA[:, m, :]),
             reads=[r_xAp[m % 2]], writes=[r_x2bfc])
        stats_full(m, NCH - 1, x2bfc, r_x2bfc)
    stats_full_finish()

    otm = [t3sc.sb(f"otm{i}", [128, 512], F32) for i in range(2)]
    r_otm = [res("otm0"), res("otm1")]
    for qd in range(4):
        for cc in range(4):
            m = qd * 4 + cc
            P.op("vector", lambda v, m=m: v.tensor_sub(
                xA[:, m, :], xA[:, m, :], meanF[:]), reads=[r_meanF],
                writes=[r_xAp[m % 2]])
            P.op("vector", lambda v, m=m: v.tensor_mul(
                xA[:, m, :], xA[:, m, :], invF[:]), reads=[r_invF],
                writes=[r_xAp[m % 2]])
        for blk in range(8):
            b0 = blk * 128
            ob = (qd * 8 + blk) % 2
            for cc in range(4):
                c = qd * 4 + cc
                sl = c % 2
                P.op("tensor", lambda t, c=c, b0=b0, sl=sl: t.transpose(
                    mm2[sl][:, 0:128], xA[:, c, b0:b0 + 128], ident_f[:]),
                    reads=[r_xAp[c % 2], r_const], writes=[r_mm2[sl]])
                if cc % 2 == 0:
                    P.op("scalar", lambda sc, cc=cc, sl=sl, ob=ob: sc.copy(
                        otm[ob][:, cc * 128:(cc + 1) * 128], mm2[sl][:, 0:128]),
                        reads=[r_mm2[sl]], writes=[r_otm[ob]])
                else:
                    P.op("vector", lambda v, cc=cc, sl=sl, ob=ob: v.tensor_copy(
                        otm[ob][:, cc * 128:(cc + 1) * 128], mm2[sl][:, 0:128]),
                        reads=[r_mm2[sl]], writes=[r_otm[ob]])
            P.op("sync", lambda s, b0=b0, qd=qd, ob=ob: s.dma_start(
                out=out_d[b0:b0 + 128, qd * 512:(qd + 1) * 512],
                in_=otm[ob][:]), reads=[r_otm[ob]], dma="out")

    t3sc.close()
    tsc.close()
    finish(nc, ctx, P, perm)
    return nc, tap_d


def finish(nc, ctx, P, perm):
    P.ins["sync"].append([lambda s: s.nop(),
                          [("D", k, c) for k, c in P.dmacnt.items()],
                          False, None])
    P.finalize(nc, ctx)
    with nc.Block() as block:
        @block.tensor
        def _(t):
            P.run_engine("tensor", t)

        @block.vector
        def _(v):
            P.run_engine("vector", v)

        @block.scalar
        def _(s):
            P.run_engine("scalar", s)

        @block.sync
        def _(s):
            P.run_engine("sync", s)

        @block.gpsimd
        def _(g):
            P.run_engine("gpsimd", g)
    perm.close()
    ctx.close()


def make_in_maps(inputs):
    x = np.ascontiguousarray(inputs["L0_attn_x"])

    def vecN(v, n):
        return np.ascontiguousarray(np.asarray(v, np.float32).reshape(n, 128).T)

    def wcolmajor(w):
        w = np.asarray(w, np.float32)
        K, M = w.shape
        return np.ascontiguousarray(
            w.reshape(K // 128, 128, M // 128, 128).transpose(2, 1, 0, 3))

    def wcolmajor_e(w):
        return np.ascontiguousarray(
            np.stack([wcolmajor(w[e]) for e in range(w.shape[0])]))

    shared = {
        "ln1w": vecN(inputs["L0_attn_ln1_w"], 16), "ln1b": vecN(inputs["L0_attn_ln1_b"], 16),
        "ln2w": vecN(inputs["L0_attn_ln2_w"], 16), "ln2b": vecN(inputs["L0_attn_ln2_b"], 16),
        "inb": vecN(inputs["L0_attn_in_b"], 16), "outb": vecN(inputs["L0_attn_out_b"], 16),
        "b2v": vecN(inputs["L0_attn_b2"], 16), "b1v": vecN(inputs["L0_attn_b1"], 64),
        "in_w": wcolmajor(inputs["L0_attn_in_w"]),
        "out_w": wcolmajor(inputs["L0_attn_out_w"]),
        "w1": wcolmajor(inputs["L0_attn_w1"]),
        "w2": wcolmajor(inputs["L0_attn_w2"]),
        "wsg": wcolmajor(inputs["L0_ffn_Ws_gate"]),
        "wsu": wcolmajor(inputs["L0_ffn_Ws_up"]),
        "wsd": wcolmajor(inputs["L0_ffn_Ws_down"]),
        "weg": wcolmajor_e(inputs["L0_ffn_We_gate"]),
        "weu": wcolmajor_e(inputs["L0_ffn_We_up"]),
        "wed": wcolmajor_e(inputs["L0_ffn_We_down"]),
        "wr": np.ascontiguousarray(inputs["L0_ffn_W_router"]),
        "Am": np.ascontiguousarray(inputs["L0_attn_A"]),
        "Bm": np.ascontiguousarray(inputs["L0_attn_Bm"]),
        "Cm": np.ascontiguousarray(inputs["L0_attn_Cm"]),
        "Dm": np.ascontiguousarray(inputs["L0_attn_Dm"]),
    }
    def fmaj(a):
        nt = a.shape[0] // 128
        return np.ascontiguousarray(
            a.reshape(nt, 128, NCH, 128).transpose(0, 3, 2, 1))

    in_maps = []
    for c in range(8):
        b, q = c // 4, c % 4
        t0 = q * TC
        xo = fmaj(x[b, t0:t0 + TC])
        xpad = np.zeros((TP, D), dtype=np.float32)
        lo = t0 - TP
        v0 = max(0, -lo)
        if t0 > 0:
            xpad[v0:] = x[b, max(0, lo):t0]
        mask = np.zeros((1, NGP), dtype=np.float32)
        mask[0, v0 // L:] = 1.0
        m = dict(shared)
        m.update({"xo": xo, "xp": fmaj(xpad), "maskp": mask})
        in_maps.append(m)
    return in_maps


def kernel(**inputs):
    nc, _ = build_nc(stage="full")
    in_maps = make_in_maps(inputs)
    res = run_bass_kernel_spmd(nc, in_maps, core_ids=list(range(8)))
    out = np.zeros((B, T, D), dtype=np.float32)
    for c in range(8):
        b, q = c // 4, c % 4
        out[b, q * TC:(q + 1) * TC] = res.results[c]["out"]
    return out
